# version 03
# baseline (speedup 1.0000x reference)
import contextlib
import numpy as np
import ml_dtypes
import concourse.bass as bass
import concourse.mybir as mybir
from concourse.bass_utils import run_bass_kernel_spmd

F32 = mybir.dt.float32
BF16 = mybir.dt.bfloat16
I32 = mybir.dt.int32
AF = mybir.ActivationFunctionType
ALU = mybir.AluOpType
BF = ml_dtypes.bfloat16

D = 4096
PROJ = 8512
CONV = 4352
NH = 64
HD = 64
S = 128
Q = 128
NC = 8
TK = 1024
HALO = 8
T = TK + HALO
NCH = 8
EPS = 1e-5

NF4 = np.array([-1.0, -0.6961928009986877, -0.5250730514526367, -0.39491748809814453,
                -0.28444138169288635, -0.18477343022823334, -0.09105003625154495, 0.0,
                0.07958029955625534, 0.16093020141124725, 0.24611230194568634,
                0.33791524171829224, 0.44070982933044434, 0.5626170039176941,
                0.7229568362236023, 1.0], dtype=np.float64)
DELTA = np.diff(NF4)

TSPLITS = [(0, 344), (344, 344), (688, 344)]


class P:
    def __init__(self):
        self.ops = []
        self.cur = 0

    def op(self, eng, fn, inc=1):
        self.ops.append((eng, fn, self.cur, inc))
        self.cur += inc

    def finalize(self, nc, block, sem):
        for name in ["sync", "gpsimd", "vector", "scalar", "tensor"]:
            mine = [(f, at, inc) for (e, f, at, inc) in self.ops if e == name]
            if not mine and name != "gpsimd":
                continue
            deco = getattr(block, name)
            def body(eng, mine=mine, name=name):
                for f, at, inc in mine:
                    eng.wait_ge(sem, at)
                    f(eng).then_inc(sem, inc)
                if name == "gpsimd":
                    eng.wait_ge(sem, self.cur)
            deco(body)


def _run(nc, in_maps):
    res = run_bass_kernel_spmd(nc, in_maps, core_ids=list(range(NC)))
    return res.results


def to_blocked(x2d, tile=128):
    K, N = x2d.shape
    kt = K // tile
    return np.ascontiguousarray(x2d.reshape(kt, tile, N).transpose(1, 0, 2).reshape(tile, kt * N))


def from_blocked(b, K, N, tile=128):
    kt = K // tile
    return np.ascontiguousarray(b.reshape(tile, kt, N).transpose(1, 0, 2).reshape(K, N))


def build_deq():
    nc = bass.Bass("TRN2", target_bir_lowering=False)
    widx = nc.dram_tensor("widx", [512, D], I32, kind="ExternalInput").ap()
    amx = nc.dram_tensor("amx", [512, D], F32, kind="ExternalInput").ap()
    wout = nc.dram_tensor("wout", [512, D], BF16, kind="ExternalOutput").ap()
    with contextlib.ExitStack() as _es:
        idx = _es.enter_context(nc.sbuf_tensor("idx", [128, D], BF16))
        am = _es.enter_context(nc.sbuf_tensor("am", [128, D], F32))
        acc = _es.enter_context(nc.sbuf_tensor("acc", [128, D], F32))
        tmp = _es.enter_context(nc.sbuf_tensor("tmp", [128, D], F32))
        w = _es.enter_context(nc.sbuf_tensor("w", [128, D], BF16))
        sem = _es.enter_context(nc.semaphore("sem"))
        block = _es.enter_context(nc.Block())
        p = P()
        for i in range(4):
            p.op("gpsimd", lambda g, i=i: g.dma_start(out=idx[:], in_=widx[i*128:(i+1)*128, :]), 16)
            p.op("sync", lambda s, i=i: s.dma_start(out=am[:], in_=amx[i*128:(i+1)*128, :]), 16)
            p.op("vector", lambda v: v.tensor_scalar(acc[:], idx[:], 0.5, float(DELTA[0]), ALU.is_ge, ALU.mult))
            for k in range(2, 16):
                p.op("vector", lambda v, k=k: v.tensor_scalar(tmp[:], idx[:], k - 0.5, float(DELTA[k-1]), ALU.is_ge, ALU.mult))
                p.op("vector", lambda v: v.tensor_tensor(acc[:], acc[:], tmp[:], ALU.add))
            p.op("vector", lambda v: v.scalar_tensor_tensor(w[:], acc[:], -1.0, am[:], ALU.add, ALU.mult))
            p.op("sync", lambda s, i=i: s.dma_start(out=wout[i*128:(i+1)*128, :], in_=w[:]), 16)
        p.finalize(nc, block, sem)
    return nc


def build_ln(mode):
    nc = bass.Bass("TRN2", target_bir_lowering=False)
    xin = nc.dram_tensor("xin", [T, D], F32, kind="ExternalInput").ap()
    if mode == "lnrms":
        lnwb = nc.dram_tensor("lnwb", [128, D], F32, kind="ExternalInput").ap()
    xout = nc.dram_tensor("xout", [T, D], BF16, kind="ExternalOutput").ap()
    ntile = [(i * 128, 128) for i in range(8)] + [(1024, 8)]
    with contextlib.ExitStack() as _es:
        xt = _es.enter_context(nc.sbuf_tensor("xt", [128, D], F32))
        sq = _es.enter_context(nc.sbuf_tensor("sq", [128, D], F32))
        lw = _es.enter_context(nc.sbuf_tensor("lw", [128, D], F32))
        st = _es.enter_context(nc.sbuf_tensor("st", [128, 8], F32))
        ot = _es.enter_context(nc.sbuf_tensor("ot", [128, D], BF16))
        sem = _es.enter_context(nc.semaphore("sem"))
        block = _es.enter_context(nc.Block())
        p = P()
        if mode == "lnrms":
            p.op("sync", lambda s: s.dma_start(out=lw[:], in_=lnwb), 16)
        for (r0, rn) in ntile:
            p.op("sync", lambda s, r0=r0, rn=rn: s.dma_start(out=xt[0:rn, :], in_=xin[r0:r0+rn, :]), 16)
            p.op("vector", lambda v, rn=rn: v.reduce_sum(st[0:rn, 0:1], xt[0:rn, :], axis=mybir.AxisListType.X))
            p.op("vector", lambda v, rn=rn: v.tensor_scalar(st[0:rn, 1:2], st[0:rn, 0:1], 1.0 / D, None, ALU.mult))
            p.op("vector", lambda v, rn=rn: v.tensor_scalar(xt[0:rn, :], xt[0:rn, :], st[0:rn, 1:2], None, ALU.subtract))
            p.op("vector", lambda v, rn=rn: v.tensor_tensor(sq[0:rn, :], xt[0:rn, :], xt[0:rn, :], ALU.mult))
            p.op("vector", lambda v, rn=rn: v.reduce_sum(st[0:rn, 2:3], sq[0:rn, :], axis=mybir.AxisListType.X))
            p.op("vector", lambda v, rn=rn: v.tensor_scalar(st[0:rn, 3:4], st[0:rn, 2:3], 1.0 / D, EPS, ALU.mult, ALU.add))
            p.op("scalar", lambda sc, rn=rn: sc.activation(st[0:rn, 4:5], st[0:rn, 3:4], AF.Sqrt))
            p.op("vector", lambda v, rn=rn: v.reciprocal(st[0:rn, 5:6], st[0:rn, 4:5]))
            if mode == "ln":
                p.op("vector", lambda v, rn=rn: v.tensor_scalar(ot[0:rn, :], xt[0:rn, :], st[0:rn, 5:6], None, ALU.mult))
            else:
                p.op("vector", lambda v, rn=rn: v.tensor_scalar(xt[0:rn, :], xt[0:rn, :], st[0:rn, 5:6], None, ALU.mult))
                p.op("vector", lambda v, rn=rn: v.tensor_tensor(sq[0:rn, :], xt[0:rn, :], xt[0:rn, :], ALU.mult))
                p.op("vector", lambda v, rn=rn: v.reduce_sum(st[0:rn, 2:3], sq[0:rn, :], axis=mybir.AxisListType.X))
                p.op("vector", lambda v, rn=rn: v.tensor_scalar(st[0:rn, 3:4], st[0:rn, 2:3], 1.0 / D, EPS, ALU.mult, ALU.add))
                p.op("scalar", lambda sc, rn=rn: sc.activation(st[0:rn, 4:5], st[0:rn, 3:4], AF.Sqrt))
                p.op("vector", lambda v, rn=rn: v.reciprocal(st[0:rn, 5:6], st[0:rn, 4:5]))
                p.op("vector", lambda v, rn=rn: v.tensor_scalar(xt[0:rn, :], xt[0:rn, :], st[0:rn, 5:6], None, ALU.mult))
                p.op("vector", lambda v, rn=rn: v.tensor_tensor(ot[0:rn, :], xt[0:rn, :], lw[0:rn, :], ALU.mult))
            p.op("sync", lambda s, r0=r0, rn=rn: s.dma_start(out=xout[r0:r0+rn, :], in_=ot[0:rn, :]), 16)
        p.finalize(nc, block, sem)
    return nc


def build_mm(mode):
    nc = bass.Bass("TRN2", target_bir_lowering=False)
    KT = 32
    if mode == "inproj":
        mtiles = [128] * 66 + [64]
    else:
        mtiles = [128] * 32
    NM = len(mtiles)
    wdt = BF16 if mode in ("ffn1", "ffn2") else F32
    wblk = nc.dram_tensor("wblk", [NM, 128, D], wdt, kind="ExternalInput").ap()
    xblk = nc.dram_tensor("xblk", [128, KT * T], BF16, kind="ExternalInput").ap()
    outs = {}
    if mode == "ffn1":
        res = nc.dram_tensor("res", [D, T], F32, kind="ExternalInput").ap()
        bcol = nc.dram_tensor("bcol", [128, 32], F32, kind="ExternalInput").ap()
        outs["x1"] = nc.dram_tensor("x1", [D, T], F32, kind="ExternalOutput").ap()
    elif mode == "ffn2":
        res = nc.dram_tensor("res", [D, T], F32, kind="ExternalInput").ap()
        bcol = nc.dram_tensor("bcol", [128, 32], F32, kind="ExternalInput").ap()
        outs["x3"] = nc.dram_tensor("x3", [D, T], F32, kind="ExternalOutput").ap()
    elif mode == "outproj":
        res = nc.dram_tensor("res", [D, T], F32, kind="ExternalInput").ap()
        outs["x2"] = nc.dram_tensor("x2", [D, T], F32, kind="ExternalOutput").ap()
    elif mode == "inproj":
        mcol = nc.dram_tensor("mcol", [128, 67], F32, kind="ExternalInput").ap()
        outs["zT"] = nc.dram_tensor("zT", [D, T], BF16, kind="ExternalOutput").ap()
        outs["xbcT"] = nc.dram_tensor("xbcT", [CONV, T], BF16, kind="ExternalOutput").ap()
        outs["dtraw"] = nc.dram_tensor("dtraw", [NH, T], F32, kind="ExternalOutput").ap()
    with contextlib.ExitStack() as _es:
        xs = _es.enter_context(nc.sbuf_tensor("xs", [128, KT * T], BF16))
        w0 = _es.enter_context(nc.sbuf_tensor("w0", [128, D], BF16))
        w1 = _es.enter_context(nc.sbuf_tensor("w1", [128, D], BF16))
        rt = _es.enter_context(nc.sbuf_tensor("rt", [128, T], F32))
        tm = _es.enter_context(nc.sbuf_tensor("tm", [128, T], F32))
        ob = _es.enter_context(nc.sbuf_tensor("ob", [128, T], BF16))
        of = _es.enter_context(nc.sbuf_tensor("of", [128, T], F32))
        cc = _es.enter_context(nc.sbuf_tensor("cc", [128, 67], F32))
        ps0 = _es.enter_context(nc.psum_tensor("ps0", [128, 344], F32))
        ps1 = _es.enter_context(nc.psum_tensor("ps1", [128, 344], F32))
        ps2 = _es.enter_context(nc.psum_tensor("ps2", [128, 344], F32))
        sem = _es.enter_context(nc.semaphore("sem"))
        block = _es.enter_context(nc.Block())
        ps = [ps0, ps1, ps2]
        p = P()
        p.op("sync", lambda s: s.dma_start(out=xs[:], in_=xblk), 16)
        if mode in ("ffn1", "ffn2"):
            p.op("sync", lambda s: s.dma_start(out=cc[:, 0:32], in_=bcol), 16)
        elif mode == "inproj":
            p.op("sync", lambda s: s.dma_start(out=cc[:], in_=mcol), 16)
        wb = [w0, w1]
        for m, mw in enumerate(mtiles):
            w = wb[m % 2]
            p.op("gpsimd", lambda g, m=m, w=w, mw=mw: g.dma_start(out=w[0:128, 0:mw*32], in_=wblk[m, :, 0:mw*32]), 16)
            for (t0, tn) in TSPLITS:
                i = TSPLITS.index((t0, tn))
                for kt in range(KT):
                    p.op("tensor", lambda t, w=w, kt=kt, mw=mw, t0=t0, tn=tn, i=i:
                         t.matmul(ps[i][0:mw, 0:tn], w[:, kt*mw:(kt+1)*mw], xs[:, kt*T+t0:kt*T+t0+tn],
                                  start=(kt == 0), stop=(kt == KT - 1)))
            if mode in ("ffn1", "ffn2"):
                p.op("sync", lambda s, m=m: s.dma_start(out=rt[:], in_=res[m*128:(m+1)*128, :]), 16)
                for i, (t0, tn) in enumerate(TSPLITS):
                    p.op("vector", lambda v, i=i, t0=t0, tn=tn, m=m:
                         v.tensor_scalar(tm[:, t0:t0+tn], ps[i][:, 0:tn], cc[:, m:m+1], 0.5, ALU.add, ALU.mult))
                p.op("vector", lambda v: v.tensor_tensor(of[:], tm[:], rt[:], ALU.add))
                oname = "x1" if mode == "ffn1" else "x3"
                p.op("sync", lambda s, m=m, oname=oname: s.dma_start(out=outs[oname][m*128:(m+1)*128, :], in_=of[:]), 16)
            elif mode == "outproj":
                p.op("sync", lambda s, m=m: s.dma_start(out=rt[:], in_=res[m*128:(m+1)*128, :]), 16)
                for i, (t0, tn) in enumerate(TSPLITS):
                    p.op("vector", lambda v, i=i, t0=t0, tn=tn:
                         v.tensor_tensor(of[:, t0:t0+tn], ps[i][:, 0:tn], rt[:, t0:t0+tn], ALU.add))
                p.op("sync", lambda s, m=m: s.dma_start(out=outs["x2"][m*128:(m+1)*128, :], in_=of[:]), 16)
            else:
                tgt_f32 = (m == 66)
                for i, (t0, tn) in enumerate(TSPLITS):
                    if tgt_f32:
                        p.op("vector", lambda v, i=i, t0=t0, tn=tn, m=m, mw=mw:
                             v.tensor_scalar(of[0:mw, t0:t0+tn], ps[i][0:mw, 0:tn], cc[0:mw, m:m+1], None, ALU.mult))
                    else:
                        p.op("vector", lambda v, i=i, t0=t0, tn=tn, m=m, mw=mw:
                             v.tensor_scalar(ob[0:mw, t0:t0+tn], ps[i][0:mw, 0:tn], cc[0:mw, m:m+1], None, ALU.mult))
                if m == 66:
                    p.op("sync", lambda s: s.dma_start(out=outs["dtraw"], in_=of[0:64, :]), 16)
                elif m < 32:
                    p.op("sync", lambda s, m=m: s.dma_start(out=outs["zT"][m*128:(m+1)*128, :], in_=ob[:]), 16)
                else:
                    p.op("sync", lambda s, m=m: s.dma_start(out=outs["xbcT"][(m-32)*128:(m-31)*128, :], in_=ob[:]), 16)
        p.finalize(nc, block, sem)
    return nc


def build_convdt():
    nc = bass.Bass("TRN2", target_bir_lowering=False)
    xbcT = nc.dram_tensor("xbcT", [CONV, T], BF16, kind="ExternalInput").ap()
    cw = nc.dram_tensor("cw", [128, 34 * 4], F32, kind="ExternalInput").ap()
    halo = nc.dram_tensor("halo", [128, 1], F32, kind="ExternalInput").ap()
    dtraw = nc.dram_tensor("dtraw", [NH, T], F32, kind="ExternalInput").ap()
    dtb = nc.dram_tensor("dtb", [NH, 1], F32, kind="ExternalInput").ap()
    alog = nc.dram_tensor("alog", [NH, 1], F32, kind="ExternalInput").ap()
    oxbc = nc.dram_tensor("oxbc", [CONV, TK], BF16, kind="ExternalOutput").ap()
    oacum = nc.dram_tensor("oacum", [NH, TK], F32, kind="ExternalOutput").ap()
    oeac = nc.dram_tensor("oeac", [NH, TK], F32, kind="ExternalOutput").ap()
    odtr = nc.dram_tensor("odtr", [NH, TK], F32, kind="ExternalOutput").ap()
    odt = nc.dram_tensor("odt", [NH, TK], F32, kind="ExternalOutput").ap()
    ocd = nc.dram_tensor("ocd", [NH, NCH], F32, kind="ExternalOutput").ap()
    olam = nc.dram_tensor("olam", [NH, 1], F32, kind="ExternalOutput").ap()
    with contextlib.ExitStack() as _es:
        xt = _es.enter_context(nc.sbuf_tensor("xt", [128, T], BF16))
        cwt = _es.enter_context(nc.sbuf_tensor("cwt", [128, 34 * 4], F32))
        hm = _es.enter_context(nc.sbuf_tensor("hm", [128, 1], F32))
        ac = _es.enter_context(nc.sbuf_tensor("ac", [128, TK], F32))
        ac2 = _es.enter_context(nc.sbuf_tensor("ac2", [128, TK], BF16))
        dr = _es.enter_context(nc.sbuf_tensor("dr", [NH, T], F32))
        d1 = _es.enter_context(nc.sbuf_tensor("d1", [NH, T], F32))
        d2 = _es.enter_context(nc.sbuf_tensor("d2", [NH, T], F32))
        dt = _es.enter_context(nc.sbuf_tensor("dt", [NH, TK], F32))
        aa = _es.enter_context(nc.sbuf_tensor("aa", [NH, TK], F32))
        acu = _es.enter_context(nc.sbuf_tensor("acu", [NH, TK], F32))
        zz = _es.enter_context(nc.sbuf_tensor("zz", [NH, Q], F32))
        lamc = _es.enter_context(nc.sbuf_tensor("lamc", [NH, NCH], F32))
        sc = _es.enter_context(nc.sbuf_tensor("sc", [NH, 8], F32))
        tq = _es.enter_context(nc.sbuf_tensor("tq", [NH, TK], F32))
        sem = _es.enter_context(nc.semaphore("sem"))
        block = _es.enter_context(nc.Block())
        p = P()
        p.op("sync", lambda s: s.dma_start(out=cwt[:], in_=cw), 16)
        p.op("sync", lambda s: s.dma_start(out=hm[:], in_=halo), 16)
        for i in range(34):
            p.op("sync", lambda s, i=i: s.dma_start(out=xt[:], in_=xbcT[i*128:(i+1)*128, :]), 16)
            p.op("vector", lambda v: v.tensor_scalar(xt[:, 0:HALO], xt[:, 0:HALO], hm[:, 0:1], None, ALU.mult))
            p.op("vector", lambda v, i=i: v.tensor_scalar(ac[:], xt[:, 5:5+TK], cwt[:, i*4:i*4+1], None, ALU.mult))
            for k in range(1, 4):
                p.op("vector", lambda v, i=i, k=k: v.scalar_tensor_tensor(
                    ac[:], xt[:, 5+k:5+k+TK], cwt[:, i*4+k:i*4+k+1], ac[:], ALU.mult, ALU.add))
            p.op("scalar", lambda sc_: sc_.activation(ac2[:], ac[:], AF.Silu))
            p.op("sync", lambda s, i=i: s.dma_start(out=oxbc[i*128:(i+1)*128, :], in_=ac2[:]), 16)
        p.op("sync", lambda s: s.dma_start(out=dr[:], in_=dtraw), 16)
        p.op("sync", lambda s: s.dma_start(out=d2[:, 0:1], in_=dtb), 16)
        p.op("sync", lambda s: s.dma_start(out=d2[:, 1:2], in_=alog), 16)
        p.op("vector", lambda v: v.tensor_scalar(dr[:], dr[:], d2[:, 0:1], None, ALU.add))
        p.op("scalar", lambda sc_: sc_.activation(d1[:], dr[:], AF.Abs))
        p.op("scalar", lambda sc_: sc_.activation(d1[:], d1[:], AF.Exp, scale=-1.0))
        p.op("scalar", lambda sc_: sc_.activation(d1[:], d1[:], AF.Ln, bias=1.0))
        p.op("vector", lambda v: v.tensor_scalar(dr[:], dr[:], 0.0, None, ALU.max))
        p.op("vector", lambda v: v.tensor_tensor(dr[:], dr[:], d1[:], ALU.add))
        p.op("vector", lambda v: v.tensor_copy(dt[:], dr[:, HALO:T]))
        p.op("scalar", lambda sc_: sc_.activation(d2[:, 2:3], d2[:, 1:2], AF.Exp))
        p.op("vector", lambda v: v.tensor_scalar(d2[:, 3:4], d2[:, 2:3], -1.0, None, ALU.mult))
        p.op("vector", lambda v: v.tensor_scalar(aa[:], dt[:], d2[:, 3:4], None, ALU.mult))
        p.op("gpsimd", lambda g: g.memset(zz[:], 0.0))
        for c in range(NCH):
            p.op("vector", lambda v, c=c: v.tensor_tensor_scan(
                acu[:, c*Q:(c+1)*Q], aa[:, c*Q:(c+1)*Q], zz[:], 0.0, ALU.add, ALU.add))
            p.op("vector", lambda v, c=c: v.tensor_copy(lamc[:, c:c+1], acu[:, (c+1)*Q-1:(c+1)*Q]))
            p.op("vector", lambda v, c=c: v.tensor_scalar(tq[:, c*Q:(c+1)*Q], acu[:, c*Q:(c+1)*Q], lamc[:, c:c+1], None, ALU.subtract))
        p.op("scalar", lambda sc_: sc_.activation(tq[:], tq[:], AF.Exp, scale=-1.0))
        p.op("vector", lambda v: v.tensor_tensor(tq[:], tq[:], dt[:], ALU.mult))
        p.op("sync", lambda s: s.dma_start(out=odtr, in_=tq[:]), 16)
        p.op("sync", lambda s: s.dma_start(out=odt, in_=dt[:]), 16)
        p.op("sync", lambda s: s.dma_start(out=oacum, in_=acu[:]), 16)
        p.op("scalar", lambda sc_: sc_.activation(aa[:], acu[:], AF.Exp))
        p.op("sync", lambda s: s.dma_start(out=oeac, in_=aa[:]), 16)
        p.op("scalar", lambda sc_: sc_.activation(sc[:, 0:NCH], lamc[:], AF.Exp))
        p.op("sync", lambda s: s.dma_start(out=ocd, in_=sc[:, 0:NCH]), 16)
        p.op("vector", lambda v: v.reduce_sum(sc[:, 0:1], lamc[:], axis=mybir.AxisListType.X))
        p.op("sync", lambda s: s.dma_start(out=olam, in_=sc[:, 0:1]), 16)
        p.finalize(nc, block, sem)
    return nc


def build_scan():
    nc = bass.Bass("TRN2", target_bir_lowering=False)
    xsn = nc.dram_tensor("xsn", [NCH, 128, D], BF16, kind="ExternalInput").ap()
    xst = nc.dram_tensor("xst", [D, TK], BF16, kind="ExternalInput").ap()
    bn = nc.dram_tensor("bn", [NCH, 128, S], BF16, kind="ExternalInput").ap()
    bt = nc.dram_tensor("bt", [S, TK], BF16, kind="ExternalInput").ap()
    ct = nc.dram_tensor("ct", [S, TK], BF16, kind="ExternalInput").ap()
    acn = nc.dram_tensor("acn", [NCH, 128, NH], F32, kind="ExternalInput").ap()
    dtn = nc.dram_tensor("dtn", [NCH, 128, NH], F32, kind="ExternalInput").ap()
    drn = nc.dram_tensor("drn", [NCH, 128, NH], F32, kind="ExternalInput").ap()
    acb = nc.dram_tensor("acb", [NCH, 128, NH * Q], F32, kind="ExternalInput").ap()
    cdb = nc.dram_tensor("cdb", [128, NCH * NH], F32, kind="ExternalInput").ap()
    dpr = nc.dram_tensor("dpr", [64, 64], F32, kind="ExternalInput").ap()
    tri = nc.dram_tensor("tri", [128, 128], BF16, kind="ExternalInput").ap()
    oy = nc.dram_tensor("oy", [D, TK], BF16, kind="ExternalOutput").ap()
    osl = nc.dram_tensor("osl", [NCH, 128, D], BF16, kind="ExternalOutput").ap()
    ohe = nc.dram_tensor("ohe", [128, D], F32, kind="ExternalOutput").ap()
    with contextlib.ExitStack() as _es:
        xsc = _es.enter_context(nc.sbuf_tensor("xsc", [128, D], BF16))
        xtc = _es.enter_context(nc.sbuf_tensor("xtc", [64, Q], BF16))
        bnc = _es.enter_context(nc.sbuf_tensor("bnc", [128, S], BF16))
        btc = _es.enter_context(nc.sbuf_tensor("btc", [S, Q], BF16))
        ctc = _es.enter_context(nc.sbuf_tensor("ctc", [S, Q], BF16))
        acnc = _es.enter_context(nc.sbuf_tensor("acnc", [128, NH], F32))
        dtnc = _es.enter_context(nc.sbuf_tensor("dtnc", [128, NH], F32))
        drnc = _es.enter_context(nc.sbuf_tensor("drnc", [128, NH], F32))
        acbc = _es.enter_context(nc.sbuf_tensor("acbc", [128, NH * Q], F32))
        cdbt = _es.enter_context(nc.sbuf_tensor("cdbt", [128, NCH * NH], F32))
        dprt = _es.enter_context(nc.sbuf_tensor("dprt", [64, 64], F32))
        trit = _es.enter_context(nc.sbuf_tensor("trit", [128, 128], BF16))
        g0m = _es.enter_context(nc.sbuf_tensor("g0m", [128, 128], BF16))
        t1 = _es.enter_context(nc.sbuf_tensor("t1", [128, Q], F32))
        eb = _es.enter_context(nc.sbuf_tensor("eb", [128, Q], BF16))
        gam = _es.enter_context(nc.sbuf_tensor("gam", [128, Q], BF16))
        xtl = _es.enter_context(nc.sbuf_tensor("xtl", [128, HD], BF16))
        slc = _es.enter_context(nc.sbuf_tensor("slc", [128, D], BF16))
        hcur = _es.enter_context(nc.sbuf_tensor("hcur", [128, D], F32))
        yt = _es.enter_context(nc.sbuf_tensor("yt", [64, Q], BF16))
        psg = _es.enter_context(nc.psum_tensor("psg", [128, 128], F32))
        psy = _es.enter_context(nc.psum_tensor("psy", [64, 128], F32))
        pss = _es.enter_context(nc.psum_tensor("pss", [128, 512], F32))
        sem = _es.enter_context(nc.semaphore("sem"))
        block = _es.enter_context(nc.Block())
        p = P()
        p.op("sync", lambda s: s.dma_start(out=cdbt[:], in_=cdb), 16)
        p.op("sync", lambda s: s.dma_start(out=dprt[:], in_=dpr), 16)
        p.op("sync", lambda s: s.dma_start(out=trit[:], in_=tri), 16)
        p.op("gpsimd", lambda g: g.memset(hcur[:], 0.0))
        for c in range(NCH):
            p.op("sync", lambda s, c=c: s.dma_start(out=xsc[:], in_=xsn[c, :, :]), 16)
            p.op("sync", lambda s, c=c: s.dma_start(out=bnc[:], in_=bn[c, :, :]), 16)
            p.op("sync", lambda s, c=c: s.dma_start(out=btc[:], in_=bt[:, c*Q:(c+1)*Q]), 16)
            p.op("sync", lambda s, c=c: s.dma_start(out=ctc[:], in_=ct[:, c*Q:(c+1)*Q]), 16)
            p.op("sync", lambda s, c=c: s.dma_start(out=acnc[:], in_=acn[c, :, :]), 16)
            p.op("sync", lambda s, c=c: s.dma_start(out=dtnc[:], in_=dtn[c, :, :]), 16)
            p.op("sync", lambda s, c=c: s.dma_start(out=drnc[:], in_=drn[c, :, :]), 16)
            p.op("sync", lambda s, c=c: s.dma_start(out=acbc[:], in_=acb[c, :, :]), 16)
            p.op("tensor", lambda t: t.matmul(psg[:], btc[:], ctc[:], start=True, stop=True))
            p.op("vector", lambda v: v.tensor_tensor(g0m[:], psg[:], trit[:], ALU.mult))
            for h in range(NH):
                p.op("vector", lambda v, h=h: v.tensor_scalar(
                    t1[:], acbc[:, h*Q:(h+1)*Q], acnc[:, h:h+1], 0.0, ALU.subtract, ALU.min))
                p.op("scalar", lambda sc_: sc_.activation(eb[:], t1[:], AF.Exp))
                p.op("vector", lambda v, h=h: v.scalar_tensor_tensor(
                    gam[:], eb[:], dtnc[:, h:h+1], g0m[:], ALU.mult, ALU.mult))
                p.op("tensor", lambda t, h=h: t.matmul(
                    psy[:], xsc[:, h*HD:(h+1)*HD], gam[:], start=True, stop=True))
                p.op("vector", lambda v, h=h: v.tensor_scalar(
                    xtl[:], xsc[:, h*HD:(h+1)*HD], drnc[:, h:h+1], None, ALU.mult))
                p.op("tensor", lambda t, h=h: t.matmul(
                    pss[:, (h % 8)*HD:(h % 8 + 1)*HD], bnc[:], xtl[:], start=True, stop=True))
                if h % 8 == 7:
                    bk = h // 8
                    p.op("vector", lambda v, bk=bk: v.tensor_copy(slc[:, bk*512:(bk+1)*512], pss[:]))
                p.op("sync", lambda s, c=c, h=h: s.dma_start(out=xtc[:], in_=xst[h*64:(h+1)*64, c*Q:(c+1)*Q]), 16)
                p.op("vector", lambda v, h=h: v.scalar_tensor_tensor(
                    yt[:], xtc[:], dprt[:, h:h+1], psy[:], ALU.mult, ALU.add))
                p.op("sync", lambda s, c=c, h=h: s.dma_start(out=oy[h*64:(h+1)*64, c*Q:(c+1)*Q], in_=yt[:]), 16)
            p.op("sync", lambda s, c=c: s.dma_start(out=osl[c, :, :], in_=slc[:]), 16)
            for h in range(NH):
                p.op("vector", lambda v, c=c, h=h: v.scalar_tensor_tensor(
                    hcur[:, h*HD:(h+1)*HD], hcur[:, h*HD:(h+1)*HD], cdbt[:, c*NH+h:c*NH+h+1],
                    slc[:, h*HD:(h+1)*HD], ALU.mult, ALU.add))
        p.op("sync", lambda s: s.dma_start(out=ohe, in_=hcur[:]), 16)
        p.finalize(nc, block, sem)
    return nc


def build_inter():
    nc = bass.Bass("TRN2", target_bir_lowering=False)
    yin = nc.dram_tensor("yin", [D, TK], BF16, kind="ExternalInput").ap()
    sl = nc.dram_tensor("sl", [NCH, 128, D], BF16, kind="ExternalInput").ap()
    ct = nc.dram_tensor("ct", [S, TK], BF16, kind="ExternalInput").ap()
    eab = nc.dram_tensor("eab", [NCH, 128, NH * Q], F32, kind="ExternalInput").ap()
    cdb = nc.dram_tensor("cdb", [128, NCH * NH], F32, kind="ExternalInput").ap()
    hp = nc.dram_tensor("hp", [3, 128, D], F32, kind="ExternalInput").ap()
    lm = nc.dram_tensor("lm", [3, NH, 1], F32, kind="ExternalInput").ap()
    um = nc.dram_tensor("um", [3, 4, NH, 1], F32, kind="ExternalInput").ap()
    oy = nc.dram_tensor("oy", [D, TK], BF16, kind="ExternalOutput").ap()
    with contextlib.ExitStack() as _es:
        yc = _es.enter_context(nc.sbuf_tensor("yc", [64, Q], BF16))
        slc = _es.enter_context(nc.sbuf_tensor("slc", [128, D], BF16))
        ctc = _es.enter_context(nc.sbuf_tensor("ctc", [S, Q], BF16))
        eabc = _es.enter_context(nc.sbuf_tensor("eabc", [128, NH * Q], F32))
        cdbt = _es.enter_context(nc.sbuf_tensor("cdbt", [128, NCH * NH], F32))
        hk = _es.enter_context(nc.sbuf_tensor("hk", [128, D], F32))
        hcur = _es.enter_context(nc.sbuf_tensor("hcur", [128, D], F32))
        hbf = _es.enter_context(nc.sbuf_tensor("hbf", [128, D], BF16))
        lmk = _es.enter_context(nc.sbuf_tensor("lmk", [NH, 8], F32))
        umk = _es.enter_context(nc.sbuf_tensor("umk", [NH, 4], F32))
        wk = _es.enter_context(nc.sbuf_tensor("wk", [NH, 4], F32))
        wrow = _es.enter_context(nc.sbuf_tensor("wrow", [1, NH], F32))
        wb = _es.enter_context(nc.sbuf_tensor("wb", [128, NH], F32))
        cte = _es.enter_context(nc.sbuf_tensor("cte", [S, Q], BF16))
        psy = _es.enter_context(nc.psum_tensor("psy", [64, 128], F32))
        sem = _es.enter_context(nc.semaphore("sem"))
        block = _es.enter_context(nc.Block())
        wrow_dram = nc.dram_tensor("wrow_dram", [NH, 1], F32, kind="Internal").ap()
        p = P()
        p.op("sync", lambda s: s.dma_start(out=cdbt[:], in_=cdb), 16)
        p.op("gpsimd", lambda g: g.memset(hcur[:], 0.0))
        for k in range(3):
            p.op("sync", lambda s, k=k: s.dma_start(out=lmk[:, k:k+1], in_=lm[k, :, :]), 16)
        for k in range(3):
            p.op("gpsimd", lambda g: g.memset(wk[:, 0:1], 0.0))
            for j in range(3):
                p.op("sync", lambda s, k=k, j=j: s.dma_start(out=umk[:, j:j+1], in_=um[k, j, :, :]), 16)
                p.op("vector", lambda v, j=j: v.scalar_tensor_tensor(
                    wk[:, 0:1], lmk[:, j:j+1], umk[:, j:j+1], wk[:, 0:1], ALU.mult, ALU.add))
            p.op("scalar", lambda sc_: sc_.activation(wk[:, 1:2], wk[:, 0:1], AF.Exp))
            p.op("sync", lambda s, k=k: s.dma_start(out=umk[:, 3:4], in_=um[k, 3, :, :]), 16)
            p.op("vector", lambda v: v.tensor_scalar(wk[:, 2:3], wk[:, 1:2], umk[:, 3:4], None, ALU.mult))
            p.op("sync", lambda s: s.dma_start(out=wrow_dram, in_=wk[:, 2:3]), 16)
            p.op("sync", lambda s: s.dma_start(out=wb[:], in_=wrow_dram.rearrange("h one -> one h").partition_broadcast(128)), 16)
            p.op("sync", lambda s, k=k: s.dma_start(out=hk[:], in_=hp[k, :, :]), 16)
            for h in range(NH):
                p.op("vector", lambda v, h=h: v.scalar_tensor_tensor(
                    hcur[:, h*HD:(h+1)*HD], hk[:, h*HD:(h+1)*HD], wb[:, h:h+1],
                    hcur[:, h*HD:(h+1)*HD], ALU.mult, ALU.add))
        for c in range(NCH):
            p.op("sync", lambda s, c=c: s.dma_start(out=ctc[:], in_=ct[:, c*Q:(c+1)*Q]), 16)
            p.op("sync", lambda s, c=c: s.dma_start(out=eabc[:], in_=eab[c, :, :]), 16)
            p.op("sync", lambda s, c=c: s.dma_start(out=slc[:], in_=sl[c, :, :]), 16)
            p.op("vector", lambda v: v.tensor_copy(hbf[:], hcur[:]))
            for h in range(NH):
                p.op("vector", lambda v, h=h: v.tensor_tensor(
                    cte[:], ctc[:], eabc[0:S, h*Q:(h+1)*Q], ALU.mult))
                p.op("tensor", lambda t, h=h: t.matmul(
                    psy[:], hbf[:, h*HD:(h+1)*HD], cte[:], start=True, stop=True))
                p.op("sync", lambda s, c=c, h=h: s.dma_start(out=yc[:], in_=yin[h*64:(h+1)*64, c*Q:(c+1)*Q]), 16)
                p.op("vector", lambda v: v.tensor_tensor(yc[:], yc[:], psy[:], ALU.add))
                p.op("sync", lambda s, c=c, h=h: s.dma_start(out=oy[h*64:(h+1)*64, c*Q:(c+1)*Q], in_=yc[:]), 16)
            for h in range(NH):
                p.op("vector", lambda v, c=c, h=h: v.scalar_tensor_tensor(
                    hcur[:, h*HD:(h+1)*HD], hcur[:, h*HD:(h+1)*HD], cdbt[:, c*NH+h:c*NH+h+1],
                    slc[:, h*HD:(h+1)*HD], ALU.mult, ALU.add))
        p.finalize(nc, block, sem)
    return nc


def build_gate():
    nc = bass.Bass("TRN2", target_bir_lowering=False)
    yn = nc.dram_tensor("yn", [TK, D], BF16, kind="ExternalInput").ap()
    zn = nc.dram_tensor("zn", [TK, D], BF16, kind="ExternalInput").ap()
    nwb = nc.dram_tensor("nwb", [128, D], F32, kind="ExternalInput").ap()
    og = nc.dram_tensor("og", [TK, D], BF16, kind="ExternalOutput").ap()
    with contextlib.ExitStack() as _es:
        yt = _es.enter_context(nc.sbuf_tensor("yt", [128, D], BF16))
        zt = _es.enter_context(nc.sbuf_tensor("zt", [128, D], BF16))
        nw = _es.enter_context(nc.sbuf_tensor("nw", [128, D], F32))
        g0 = _es.enter_context(nc.sbuf_tensor("g0", [128, D], F32))
        sq = _es.enter_context(nc.sbuf_tensor("sq", [128, D], F32))
        st = _es.enter_context(nc.sbuf_tensor("st", [128, 8], F32))
        ot = _es.enter_context(nc.sbuf_tensor("ot", [128, D], BF16))
        sem = _es.enter_context(nc.semaphore("sem"))
        block = _es.enter_context(nc.Block())
        p = P()
        p.op("sync", lambda s: s.dma_start(out=nw[:], in_=nwb), 16)
        for i in range(8):
            p.op("sync", lambda s, i=i: s.dma_start(out=yt[:], in_=yn[i*128:(i+1)*128, :]), 16)
            p.op("sync", lambda s, i=i: s.dma_start(out=zt[:], in_=zn[i*128:(i+1)*128, :]), 16)
            p.op("scalar", lambda sc_: sc_.activation(zt[:], zt[:], AF.Silu))
            p.op("vector", lambda v: v.tensor_tensor(g0[:], yt[:], zt[:], ALU.mult))
            p.op("vector", lambda v: v.tensor_tensor(sq[:], g0[:], g0[:], ALU.mult))
            p.op("vector", lambda v: v.reduce_sum(st[:, 0:1], sq[:], axis=mybir.AxisListType.X))
            p.op("vector", lambda v: v.tensor_scalar(st[:, 1:2], st[:, 0:1], 1.0 / D, EPS, ALU.mult, ALU.add))
            p.op("scalar", lambda sc_: sc_.activation(st[:, 2:3], st[:, 1:2], AF.Sqrt))
            p.op("vector", lambda v: v.reciprocal(st[:, 3:4], st[:, 2:3]))
            p.op("vector", lambda v: v.tensor_scalar(g0[:], g0[:], st[:, 3:4], None, ALU.mult))
            p.op("vector", lambda v: v.tensor_tensor(ot[:], g0[:], nw[:], ALU.mult))
            p.op("sync", lambda s, i=i: s.dma_start(out=og[i*128:(i+1)*128, :], in_=ot[:]), 16)
        p.finalize(nc, block, sem)
    return nc


_cache = {}

def _get(name, fn):
    if name not in _cache:
        _cache[name] = fn()
    return _cache[name]


def kernel(L0_attn_hidden_states, L0_attn_residual, L0_attn_ln_w, L0_attn_in_proj_w,
           L0_attn_conv_w, L0_attn_A_log, L0_attn_dt_bias, L0_attn_D, L0_attn_norm_w,
           L0_attn_out_proj_w, L0_attn_mup_vector, L0_ffn_w_idx, L0_ffn_absmax, L0_ffn_bias):
    f32 = np.float32
    x0 = np.asarray(L0_attn_hidden_states, f32)
    widx = np.asarray(L0_ffn_w_idx, np.int32)
    absmax = np.asarray(L0_ffn_absmax, f32)
    fbias = np.asarray(L0_ffn_bias, f32)
    wp = np.asarray(L0_attn_in_proj_w, f32)
    wo = np.asarray(L0_attn_out_proj_w, f32)
    convw = np.asarray(L0_attn_conv_w, f32)
    alog = np.asarray(L0_attn_A_log, f32)
    dtb = np.asarray(L0_attn_dt_bias, f32)
    Dv = np.asarray(L0_attn_D, f32)
    lnw = np.asarray(L0_attn_ln_w, f32)
    nrmw = np.asarray(L0_attn_norm_w, f32)
    mup = np.asarray(L0_attn_mup_vector, f32)

    cores = []
    for c in range(NC):
        b, s4 = c // 4, c % 4
        t0 = s4 * TK
        idxs = (np.arange(t0 - HALO, t0 + TK)) % 4096
        cores.append((b, t0, idxs))

    nc1 = _get("deq", build_deq)
    amx_full = np.repeat(absmax.reshape(D, 64), 64, axis=1).astype(f32)
    maps = []
    for c in range(NC):
        sl = slice(c * 512, (c + 1) * 512)
        maps.append({"widx": np.ascontiguousarray(widx[sl]), "amx": np.ascontiguousarray(amx_full[sl])})
    r = _run(nc1, maps)
    Wf = np.concatenate([np.asarray(r[c]["wout"]) for c in range(NC)], axis=0)
    WT = np.ascontiguousarray(Wf.T)
    wt_blk = np.stack([to_blocked(WT[:, m*128:(m+1)*128]) for m in range(32)])

    ncln = _get("ln", lambda: build_ln("ln"))
    maps = [{"xin": np.ascontiguousarray(x0[b][ix])} for (b, t0, ix) in cores]
    r = _run(ncln, maps)
    xln1 = [np.asarray(r[c]["xout"]) for c in range(NC)]

    def pack_act(xn):
        return to_blocked(np.ascontiguousarray(xn.T))

    bcol = np.zeros((128, 32), f32)
    bcol[:, :] = fbias.reshape(32, 128).T

    ncm1 = _get("mm_ffn1", lambda: build_mm("ffn1"))
    maps = []
    for c, (b, t0, ix) in enumerate(cores):
        maps.append({"wblk": wt_blk, "xblk": pack_act(xln1[c]),
                     "res": np.ascontiguousarray(x0[b][ix].T), "bcol": bcol})
    r = _run(ncm1, maps)
    x1T = [np.asarray(r[c]["x1"]) for c in range(NC)]

    ncln2 = _get("lnrms", lambda: build_ln("lnrms"))
    lnwb = np.broadcast_to(lnw, (128, D)).astype(f32)
    maps = [{"xin": np.ascontiguousarray(x1T[c].T), "lnwb": lnwb} for c in range(NC)]
    r = _run(ncln2, maps)
    xm = [np.asarray(r[c]["xout"]) for c in range(NC)]

    ncm2 = _get("mm_inproj", lambda: build_mm("inproj"))
    wp_blk_list = []
    for m in range(66):
        wp_blk_list.append(to_blocked(wp[:, m*128:(m+1)*128]))
    wp_blk_list.append(np.pad(to_blocked(wp[:, 8448:8512]), ((0, 0), (0, 32 * 64))))
    wp_blk = np.stack([np.pad(bb, ((0, 0), (0, 4096 - bb.shape[1]))) for bb in wp_blk_list])
    mcol = np.zeros((128, 67), f32)
    mupp = np.pad(mup, (0, 67 * 128 - PROJ))
    mcol[:, :] = mupp.reshape(67, 128).T
    maps = [{"wblk": wp_blk, "xblk": pack_act(xm[c]), "mcol": mcol} for c in range(NC)]
    r = _run(ncm2, maps)
    zT = [np.asarray(r[c]["zT"]) for c in range(NC)]
    xbcT = [np.asarray(r[c]["xbcT"]) for c in range(NC)]
    dtraw = [np.asarray(r[c]["dtraw"]) for c in range(NC)]

    nc5 = _get("convdt", build_convdt)
    cwt = np.zeros((128, 34 * 4), f32)
    cw_r = convw.reshape(34, 128, 4)
    for i in range(34):
        cwt[:, i*4:(i+1)*4] = cw_r[i]
    maps = []
    for c in range(NC):
        hm = np.full((128, 1), 0.0 if c % 4 == 0 else 1.0, f32)
        maps.append({"xbcT": xbcT[c], "cw": cwt, "halo": hm, "dtraw": dtraw[c],
                     "dtb": dtb.reshape(NH, 1), "alog": alog.reshape(NH, 1)})
    r = _run(nc5, maps)
    xbc = [np.asarray(r[c]["oxbc"]) for c in range(NC)]
    acum = [np.asarray(r[c]["oacum"]) for c in range(NC)]
    eac = [np.asarray(r[c]["oeac"]) for c in range(NC)]
    dtr = [np.asarray(r[c]["odtr"]) for c in range(NC)]
    dtv = [np.asarray(r[c]["odt"]) for c in range(NC)]
    cd = [np.asarray(r[c]["ocd"]) for c in range(NC)]
    lam = [np.asarray(r[c]["olam"]) for c in range(NC)]

    nc6 = _get("scan", build_scan)
    tri = (np.arange(128)[None, :] >= np.arange(128)[:, None]).astype(BF)
    dpr = np.ascontiguousarray(np.broadcast_to(Dv, (64, 64))).astype(f32)
    maps = []
    for c in range(NC):
        xsT = xbc[c][:D]
        bT = xbc[c][D:D+S]
        cT = xbc[c][D+S:D+2*S]
        xs_nat = np.ascontiguousarray(xsT.T).reshape(NCH, 128, D)
        b_nat = np.ascontiguousarray(bT.T).reshape(NCH, 128, S)
        ac_nat = np.ascontiguousarray(acum[c].T).astype(f32).reshape(NCH, 128, NH)
        dt_nat = np.ascontiguousarray(dtv[c].T).astype(f32).reshape(NCH, 128, NH)
        dr_nat = np.ascontiguousarray(dtr[c].T).astype(f32).reshape(NCH, 128, NH)
        acb = np.broadcast_to(acum[c].reshape(NH, NCH, Q).transpose(1, 0, 2).reshape(NCH, 1, NH * Q),
                              (NCH, 128, NH * Q)).astype(f32)
        cdb = np.broadcast_to(cd[c].T.reshape(1, NCH * NH), (128, NCH * NH)).astype(f32)
        maps.append({"xsn": np.ascontiguousarray(xs_nat), "xst": np.ascontiguousarray(xsT),
                     "bn": np.ascontiguousarray(b_nat), "bt": np.ascontiguousarray(bT),
                     "ct": np.ascontiguousarray(cT),
                     "acn": ac_nat, "dtn": dt_nat, "drn": dr_nat,
                     "acb": np.ascontiguousarray(acb), "cdb": np.ascontiguousarray(cdb),
                     "dpr": dpr, "tri": tri})
    r = _run(nc6, maps)
    yintra = [np.asarray(r[c]["oy"]) for c in range(NC)]
    sloc = [np.asarray(r[c]["osl"]) for c in range(NC)]
    hend = [np.asarray(r[c]["ohe"]) for c in range(NC)]

    nc7 = _get("inter", build_inter)
    maps = []
    for c in range(NC):
        g = c % 4
        base = (c // 4) * 4
        hp = np.zeros((3, 128, D), f32)
        lmv = np.zeros((3, NH, 1), f32)
        um = np.zeros((3, 4, NH, 1), f32)
        for k in range(3):
            if k < g:
                hp[k] = hend[base + k]
                lmv[k] = lam[base + k].reshape(NH, 1)
                um[k, 3] = 1.0
                for j in range(3):
                    if k < j < g:
                        um[k, j] = 1.0
        cT = xbc[c][D+S:D+2*S]
        eab = np.broadcast_to(eac[c].reshape(NH, NCH, Q).transpose(1, 0, 2).reshape(NCH, 1, NH * Q),
                              (NCH, 128, NH * Q)).astype(f32)
        cdb = np.broadcast_to(cd[c].T.reshape(1, NCH * NH), (128, NCH * NH)).astype(f32)
        maps.append({"yin": yintra[c], "sl": sloc[c], "ct": np.ascontiguousarray(cT),
                     "eab": np.ascontiguousarray(eab), "cdb": np.ascontiguousarray(cdb),
                     "hp": hp, "lm": lmv, "um": um})
    r = _run(nc7, maps)
    ytot = [np.asarray(r[c]["oy"]) for c in range(NC)]

    nc8 = _get("gate", build_gate)
    nwb = np.broadcast_to(nrmw, (128, D)).astype(f32)
    maps = []
    for c in range(NC):
        maps.append({"yn": np.ascontiguousarray(ytot[c].T), "zn": np.ascontiguousarray(zT[c][:, HALO:].T),
                     "nwb": nwb})
    r = _run(nc8, maps)
    gn = [np.asarray(r[c]["og"]) for c in range(NC)]

    ncm3 = _get("mm_outproj", lambda: build_mm("outproj"))
    wo_blk = np.stack([to_blocked(wo[:, m*128:(m+1)*128]) for m in range(32)])
    maps = []
    for c in range(NC):
        gfull = np.zeros((T, D), BF)
        gfull[HALO:] = gn[c]
        maps.append({"wblk": wo_blk, "xblk": pack_act(gfull), "res": x1T[c]})
    r = _run(ncm3, maps)
    x2T = [np.asarray(r[c]["x2"]) for c in range(NC)]

    maps = [{"xin": np.ascontiguousarray(x2T[c].T)} for c in range(NC)]
    r = _run(ncln, maps)
    xln3 = [np.asarray(r[c]["xout"]) for c in range(NC)]

    ncm4 = _get("mm_ffn2", lambda: build_mm("ffn2"))
    maps = []
    for c in range(NC):
        maps.append({"wblk": wt_blk, "xblk": pack_act(xln3[c]), "res": x2T[c], "bcol": bcol})
    r = _run(ncm4, maps)
    x3T = [np.asarray(r[c]["x3"]) for c in range(NC)]

    out = np.empty((2, 4096, D), f32)
    for c, (b, t0, ix) in enumerate(cores):
        out[b, t0:t0+TK] = x3T[c][:, HALO:].T
    return out


# version 05
# speedup vs baseline: 1.7869x; 1.7869x over previous
import contextlib
import numpy as np
import ml_dtypes
import concourse.bass as bass
import concourse.mybir as mybir
from concourse.bass_utils import run_bass_kernel_spmd

F32 = mybir.dt.float32
BF16 = mybir.dt.bfloat16
I32 = mybir.dt.int32
AF = mybir.ActivationFunctionType
ALU = mybir.AluOpType
BF = ml_dtypes.bfloat16

D = 4096
PROJ = 8512
CONV = 4352
NH = 64
HD = 64
S = 128
Q = 128
NC = 8
TK = 1024
HALO = 8
T = TK + HALO
NCH = 8
EPS = 1e-5

NF4 = np.array([-1.0, -0.6961928009986877, -0.5250730514526367, -0.39491748809814453,
                -0.28444138169288635, -0.18477343022823334, -0.09105003625154495, 0.0,
                0.07958029955625534, 0.16093020141124725, 0.24611230194568634,
                0.33791524171829224, 0.44070982933044434, 0.5626170039176941,
                0.7229568362236023, 1.0], dtype=np.float64)
DELTA = np.diff(NF4)

TSPLITS = [(0, 344), (344, 344), (688, 344)]


class P:
    def __init__(self):
        self.ops = []
        self.cur = 0

    def op(self, eng, fn, inc=1):
        self.ops.append((eng, fn, self.cur, inc))
        self.cur += inc

    def finalize(self, nc, block, sem):
        for name in ["sync", "gpsimd", "vector", "scalar", "tensor"]:
            mine = [(f, at, inc) for (e, f, at, inc) in self.ops if e == name]
            if not mine and name != "gpsimd":
                continue
            deco = getattr(block, name)
            def body(eng, mine=mine, name=name):
                for f, at, inc in mine:
                    eng.wait_ge(sem, at)
                    f(eng).then_inc(sem, inc)
                if name == "gpsimd":
                    eng.wait_ge(sem, self.cur)
            deco(body)


def _run(nc, in_maps):
    res = run_bass_kernel_spmd(nc, in_maps, core_ids=list(range(NC)))
    return res.results


def to_blocked(x2d, tile=128):
    K, N = x2d.shape
    kt = K // tile
    return np.ascontiguousarray(x2d.reshape(kt, tile, N).transpose(1, 0, 2).reshape(tile, kt * N))


def from_blocked(b, K, N, tile=128):
    kt = K // tile
    return np.ascontiguousarray(b.reshape(tile, kt, N).transpose(1, 0, 2).reshape(K, N))


def build_deq():
    nc = bass.Bass("TRN2", target_bir_lowering=False)
    widx = nc.dram_tensor("widx", [512, D], I32, kind="ExternalInput").ap()
    amx = nc.dram_tensor("amx", [512, D], F32, kind="ExternalInput").ap()
    wout = nc.dram_tensor("wout", [512, D], BF16, kind="ExternalOutput").ap()
    with contextlib.ExitStack() as _es:
        idx = _es.enter_context(nc.sbuf_tensor("idx", [128, D], BF16))
        am = _es.enter_context(nc.sbuf_tensor("am", [128, D], F32))
        acc = _es.enter_context(nc.sbuf_tensor("acc", [128, D], F32))
        tmp = _es.enter_context(nc.sbuf_tensor("tmp", [128, D], F32))
        w = _es.enter_context(nc.sbuf_tensor("w", [128, D], BF16))
        sem = _es.enter_context(nc.semaphore("sem"))
        block = _es.enter_context(nc.Block())
        p = P()
        for i in range(4):
            p.op("gpsimd", lambda g, i=i: g.dma_start(out=idx[:], in_=widx[i*128:(i+1)*128, :]), 16)
            p.op("sync", lambda s, i=i: s.dma_start(out=am[:], in_=amx[i*128:(i+1)*128, :]), 16)
            p.op("vector", lambda v: v.tensor_scalar(acc[:], idx[:], 0.5, float(DELTA[0]), ALU.is_ge, ALU.mult))
            for k in range(2, 16):
                p.op("vector", lambda v, k=k: v.tensor_scalar(tmp[:], idx[:], k - 0.5, float(DELTA[k-1]), ALU.is_ge, ALU.mult))
                p.op("vector", lambda v: v.tensor_tensor(acc[:], acc[:], tmp[:], ALU.add))
            p.op("vector", lambda v: v.scalar_tensor_tensor(w[:], acc[:], -1.0, am[:], ALU.add, ALU.mult))
            p.op("sync", lambda s, i=i: s.dma_start(out=wout[i*128:(i+1)*128, :], in_=w[:]), 16)
        p.finalize(nc, block, sem)
    return nc


def build_ln(mode):
    nc = bass.Bass("TRN2", target_bir_lowering=False)
    xin = nc.dram_tensor("xin", [T, D], F32, kind="ExternalInput").ap()
    if mode == "lnrms":
        lnwb = nc.dram_tensor("lnwb", [128, D], F32, kind="ExternalInput").ap()
    xout = nc.dram_tensor("xout", [T, D], BF16, kind="ExternalOutput").ap()
    ntile = [(i * 128, 128) for i in range(8)] + [(1024, 8)]
    with contextlib.ExitStack() as _es:
        xt = _es.enter_context(nc.sbuf_tensor("xt", [128, D], F32))
        sq = _es.enter_context(nc.sbuf_tensor("sq", [128, D], F32))
        lw = _es.enter_context(nc.sbuf_tensor("lw", [128, D], F32))
        st = _es.enter_context(nc.sbuf_tensor("st", [128, 8], F32))
        ot = _es.enter_context(nc.sbuf_tensor("ot", [128, D], BF16))
        sem = _es.enter_context(nc.semaphore("sem"))
        block = _es.enter_context(nc.Block())
        p = P()
        if mode == "lnrms":
            p.op("sync", lambda s: s.dma_start(out=lw[:], in_=lnwb), 16)
        for (r0, rn) in ntile:
            p.op("sync", lambda s, r0=r0, rn=rn: s.dma_start(out=xt[0:rn, :], in_=xin[r0:r0+rn, :]), 16)
            p.op("vector", lambda v, rn=rn: v.reduce_sum(st[0:rn, 0:1], xt[0:rn, :], axis=mybir.AxisListType.X))
            p.op("vector", lambda v, rn=rn: v.tensor_scalar(st[0:rn, 1:2], st[0:rn, 0:1], 1.0 / D, None, ALU.mult))
            p.op("vector", lambda v, rn=rn: v.tensor_scalar(xt[0:rn, :], xt[0:rn, :], st[0:rn, 1:2], None, ALU.subtract))
            p.op("vector", lambda v, rn=rn: v.tensor_tensor(sq[0:rn, :], xt[0:rn, :], xt[0:rn, :], ALU.mult))
            p.op("vector", lambda v, rn=rn: v.reduce_sum(st[0:rn, 2:3], sq[0:rn, :], axis=mybir.AxisListType.X))
            p.op("vector", lambda v, rn=rn: v.tensor_scalar(st[0:rn, 3:4], st[0:rn, 2:3], 1.0 / D, EPS, ALU.mult, ALU.add))
            p.op("scalar", lambda sc, rn=rn: sc.activation(st[0:rn, 4:5], st[0:rn, 3:4], AF.Sqrt))
            p.op("vector", lambda v, rn=rn: v.reciprocal(st[0:rn, 5:6], st[0:rn, 4:5]))
            if mode == "ln":
                p.op("vector", lambda v, rn=rn: v.tensor_scalar(ot[0:rn, :], xt[0:rn, :], st[0:rn, 5:6], None, ALU.mult))
            else:
                p.op("vector", lambda v, rn=rn: v.tensor_scalar(xt[0:rn, :], xt[0:rn, :], st[0:rn, 5:6], None, ALU.mult))
                p.op("vector", lambda v, rn=rn: v.tensor_tensor(sq[0:rn, :], xt[0:rn, :], xt[0:rn, :], ALU.mult))
                p.op("vector", lambda v, rn=rn: v.reduce_sum(st[0:rn, 2:3], sq[0:rn, :], axis=mybir.AxisListType.X))
                p.op("vector", lambda v, rn=rn: v.tensor_scalar(st[0:rn, 3:4], st[0:rn, 2:3], 1.0 / D, EPS, ALU.mult, ALU.add))
                p.op("scalar", lambda sc, rn=rn: sc.activation(st[0:rn, 4:5], st[0:rn, 3:4], AF.Sqrt))
                p.op("vector", lambda v, rn=rn: v.reciprocal(st[0:rn, 5:6], st[0:rn, 4:5]))
                p.op("vector", lambda v, rn=rn: v.tensor_scalar(xt[0:rn, :], xt[0:rn, :], st[0:rn, 5:6], None, ALU.mult))
                p.op("vector", lambda v, rn=rn: v.tensor_tensor(ot[0:rn, :], xt[0:rn, :], lw[0:rn, :], ALU.mult))
            p.op("sync", lambda s, r0=r0, rn=rn: s.dma_start(out=xout[r0:r0+rn, :], in_=ot[0:rn, :]), 16)
        p.finalize(nc, block, sem)
    return nc


def build_mm(mode):
    nc = bass.Bass("TRN2", target_bir_lowering=False)
    KT = 32
    if mode == "inproj":
        mtiles = [128] * 66 + [64]
    else:
        mtiles = [128] * 32
    NM = len(mtiles)
    has_res = mode in ("ffn1", "ffn2", "outproj")
    wdt = BF16 if mode in ("ffn1", "ffn2") else F32
    wblk = nc.dram_tensor("wblk", [NM, 128, D], wdt, kind="ExternalInput").ap()
    xblk = nc.dram_tensor("xblk", [128, KT * T], BF16, kind="ExternalInput").ap()
    outs = {}
    if mode in ("ffn1", "ffn2"):
        res = nc.dram_tensor("res", [D, T], F32, kind="ExternalInput").ap()
        bcol = nc.dram_tensor("bcol", [128, 32], F32, kind="ExternalInput").ap()
        oname = "x1" if mode == "ffn1" else "x3"
        outs[oname] = nc.dram_tensor(oname, [D, T], F32, kind="ExternalOutput").ap()
    elif mode == "outproj":
        res = nc.dram_tensor("res", [D, T], F32, kind="ExternalInput").ap()
        outs["x2"] = nc.dram_tensor("x2", [D, T], F32, kind="ExternalOutput").ap()
    else:
        mcol = nc.dram_tensor("mcol", [128, 67], F32, kind="ExternalInput").ap()
        outs["zT"] = nc.dram_tensor("zT", [D, T], BF16, kind="ExternalOutput").ap()
        outs["xbcT"] = nc.dram_tensor("xbcT", [CONV, T], BF16, kind="ExternalOutput").ap()
        outs["dtraw"] = nc.dram_tensor("dtraw", [NH, T], F32, kind="ExternalOutput").ap()
    with contextlib.ExitStack() as _es:
        xs = _es.enter_context(nc.sbuf_tensor("xs", [128, KT * T], BF16))
        w0 = _es.enter_context(nc.sbuf_tensor("w0", [128, D], BF16))
        w1 = _es.enter_context(nc.sbuf_tensor("w1", [128, D], BF16))
        rt0 = _es.enter_context(nc.sbuf_tensor("rt0", [128, T], F32))
        rt1 = _es.enter_context(nc.sbuf_tensor("rt1", [128, T], F32))
        tm0 = _es.enter_context(nc.sbuf_tensor("tm0", [128, T], F32))
        tm1 = _es.enter_context(nc.sbuf_tensor("tm1", [128, T], F32))
        ob0 = _es.enter_context(nc.sbuf_tensor("ob0", [128, T], BF16))
        ob1 = _es.enter_context(nc.sbuf_tensor("ob1", [128, T], BF16))
        of0 = _es.enter_context(nc.sbuf_tensor("of0", [128, T], F32))
        of1 = _es.enter_context(nc.sbuf_tensor("of1", [128, T], F32))
        cc = _es.enter_context(nc.sbuf_tensor("cc", [128, 67], F32))
        pss = [[_es.enter_context(nc.psum_tensor(f"ps{sdx}{i}", [128, 344], F32))
                for i in range(3)] for sdx in range(2)]
        sx = _es.enter_context(nc.semaphore("sx"))
        sw = _es.enter_context(nc.semaphore("sw"))
        smm = _es.enter_context(nc.semaphore("smm"))
        sev = _es.enter_context(nc.semaphore("sev"))
        srt = _es.enter_context(nc.semaphore("srt"))
        sout = _es.enter_context(nc.semaphore("sout"))
        block = _es.enter_context(nc.Block())
        wb = [w0, w1]
        rts = [rt0, rt1]
        tms = [tm0, tm1]
        obs = [ob0, ob1]
        ofs = [of0, of1]
        n_static = 1 if mode == "outproj" else 2

        @block.sync
        def _(s):
            s.dma_start(out=xs[:], in_=xblk).then_inc(sx, 16)
            if mode == "inproj":
                s.dma_start(out=cc[:], in_=mcol).then_inc(sx, 16)
            elif mode != "outproj":
                s.dma_start(out=cc[:, 0:32], in_=bcol).then_inc(sx, 16)
            if has_res:
                for m in range(NM):
                    if m >= 2:
                        s.wait_ge(sev, m - 1)
                    s.dma_start(out=rts[m % 2][:], in_=res[m*128:(m+1)*128, :]).then_inc(srt, 16)
                    if m >= 1:
                        s.wait_ge(sev, m)
                        mp = m - 1
                        s.dma_start(out=outs[list(outs)[0]][mp*128:(mp+1)*128, :], in_=ofs[mp % 2][:]).then_inc(sout, 16)
                s.wait_ge(sev, NM)
                mp = NM - 1
                s.dma_start(out=outs[list(outs)[0]][mp*128:(mp+1)*128, :], in_=ofs[mp % 2][:]).then_inc(sout, 16)
            else:
                for m, mw in enumerate(mtiles):
                    s.wait_ge(sev, m + 1)
                    if m == 66:
                        s.dma_start(out=outs["dtraw"], in_=ofs[m % 2][0:64, :]).then_inc(sout, 16)
                    elif m < 32:
                        s.dma_start(out=outs["zT"][m*128:(m+1)*128, :], in_=obs[m % 2][:]).then_inc(sout, 16)
                    else:
                        s.dma_start(out=outs["xbcT"][(m-32)*128:(m-31)*128, :], in_=obs[m % 2][:]).then_inc(sout, 16)

        @block.gpsimd
        def _(g):
            for m, mw in enumerate(mtiles):
                if m >= 2:
                    g.wait_ge(smm, m - 1)
                g.dma_start(out=wb[m % 2][0:128, 0:mw*32], in_=wblk[m, :, 0:mw*32]).then_inc(sw, 16)
            g.wait_ge(sout, 16 * NM)

        @block.tensor
        def _(t):
            t.wait_ge(sx, 16 * n_static)
            for m, mw in enumerate(mtiles):
                t.wait_ge(sw, 16 * (m + 1))
                if m >= 2:
                    t.wait_ge(sev, m - 1)
                ps = pss[m % 2]
                last = None
                for kt in range(KT):
                    for i, (t0, tn) in enumerate(TSPLITS):
                        last = t.matmul(ps[i][0:mw, 0:tn], wb[m % 2][:, kt*mw:(kt+1)*mw],
                                        xs[:, kt*T+t0:kt*T+t0+tn],
                                        start=(kt == 0), stop=(kt == KT - 1))
                last.then_inc(smm, 1)

        @block.vector
        def _(v):
            for m, mw in enumerate(mtiles):
                v.wait_ge(smm, m + 1)
                if m >= 2:
                    v.wait_ge(sout, 16 * (m - 1))
                ps = pss[m % 2]
                if mode in ("ffn1", "ffn2"):
                    v.wait_ge(srt, 16 * (m + 1))
                    for i, (t0, tn) in enumerate(TSPLITS):
                        v.tensor_scalar(tms[m % 2][:, t0:t0+tn], ps[i][:, 0:tn], cc[:, m:m+1], 0.5, ALU.add, ALU.mult)
                    v.tensor_tensor(ofs[m % 2][:], tms[m % 2][:], rts[m % 2][:], ALU.add).then_inc(sev, 1)
                elif mode == "outproj":
                    v.wait_ge(srt, 16 * (m + 1))
                    for i, (t0, tn) in enumerate(TSPLITS):
                        last = v.tensor_tensor(ofs[m % 2][:, t0:t0+tn], ps[i][:, 0:tn], rts[m % 2][:, t0:t0+tn], ALU.add)
                    last.then_inc(sev, 1)
                else:
                    tgt = ofs[m % 2] if m == 66 else obs[m % 2]
                    for i, (t0, tn) in enumerate(TSPLITS):
                        last = v.tensor_scalar(tgt[0:mw, t0:t0+tn], ps[i][0:mw, 0:tn], cc[0:mw, m:m+1], None, ALU.mult)
                    last.then_inc(sev, 1)
    return nc


def build_convdt():
    nc = bass.Bass("TRN2", target_bir_lowering=False)
    xbcT = nc.dram_tensor("xbcT", [CONV, T], BF16, kind="ExternalInput").ap()
    cw = nc.dram_tensor("cw", [128, 34 * 4], F32, kind="ExternalInput").ap()
    halo = nc.dram_tensor("halo", [128, 1], F32, kind="ExternalInput").ap()
    dtraw = nc.dram_tensor("dtraw", [NH, T], F32, kind="ExternalInput").ap()
    dtb = nc.dram_tensor("dtb", [NH, 1], F32, kind="ExternalInput").ap()
    alog = nc.dram_tensor("alog", [NH, 1], F32, kind="ExternalInput").ap()
    oxbc = nc.dram_tensor("oxbc", [CONV, TK], BF16, kind="ExternalOutput").ap()
    oacum = nc.dram_tensor("oacum", [NH, TK], F32, kind="ExternalOutput").ap()
    oeac = nc.dram_tensor("oeac", [NH, TK], F32, kind="ExternalOutput").ap()
    odtr = nc.dram_tensor("odtr", [NH, TK], F32, kind="ExternalOutput").ap()
    odt = nc.dram_tensor("odt", [NH, TK], F32, kind="ExternalOutput").ap()
    ocd = nc.dram_tensor("ocd", [NH, NCH], F32, kind="ExternalOutput").ap()
    olam = nc.dram_tensor("olam", [NH, 1], F32, kind="ExternalOutput").ap()
    with contextlib.ExitStack() as _es:
        xt = _es.enter_context(nc.sbuf_tensor("xt", [128, T], BF16))
        cwt = _es.enter_context(nc.sbuf_tensor("cwt", [128, 34 * 4], F32))
        hm = _es.enter_context(nc.sbuf_tensor("hm", [128, 1], F32))
        ac = _es.enter_context(nc.sbuf_tensor("ac", [128, TK], F32))
        ac2 = _es.enter_context(nc.sbuf_tensor("ac2", [128, TK], BF16))
        dr = _es.enter_context(nc.sbuf_tensor("dr", [NH, T], F32))
        d1 = _es.enter_context(nc.sbuf_tensor("d1", [NH, T], F32))
        d2 = _es.enter_context(nc.sbuf_tensor("d2", [NH, T], F32))
        dt = _es.enter_context(nc.sbuf_tensor("dt", [NH, TK], F32))
        aa = _es.enter_context(nc.sbuf_tensor("aa", [NH, TK], F32))
        acu = _es.enter_context(nc.sbuf_tensor("acu", [NH, TK], F32))
        zz = _es.enter_context(nc.sbuf_tensor("zz", [NH, Q], F32))
        lamc = _es.enter_context(nc.sbuf_tensor("lamc", [NH, NCH], F32))
        sc = _es.enter_context(nc.sbuf_tensor("sc", [NH, 8], F32))
        tq = _es.enter_context(nc.sbuf_tensor("tq", [NH, TK], F32))
        sem = _es.enter_context(nc.semaphore("sem"))
        block = _es.enter_context(nc.Block())
        p = P()
        p.op("sync", lambda s: s.dma_start(out=cwt[:], in_=cw), 16)
        p.op("sync", lambda s: s.dma_start(out=hm[:], in_=halo), 16)
        for i in range(34):
            p.op("sync", lambda s, i=i: s.dma_start(out=xt[:], in_=xbcT[i*128:(i+1)*128, :]), 16)
            p.op("vector", lambda v: v.tensor_scalar(xt[:, 0:HALO], xt[:, 0:HALO], hm[:, 0:1], None, ALU.mult))
            p.op("vector", lambda v, i=i: v.tensor_scalar(ac[:], xt[:, 5:5+TK], cwt[:, i*4:i*4+1], None, ALU.mult))
            for k in range(1, 4):
                p.op("vector", lambda v, i=i, k=k: v.scalar_tensor_tensor(
                    ac[:], xt[:, 5+k:5+k+TK], cwt[:, i*4+k:i*4+k+1], ac[:], ALU.mult, ALU.add))
            p.op("scalar", lambda sc_: sc_.activation(ac2[:], ac[:], AF.Silu))
            p.op("sync", lambda s, i=i: s.dma_start(out=oxbc[i*128:(i+1)*128, :], in_=ac2[:]), 16)
        p.op("sync", lambda s: s.dma_start(out=dr[:], in_=dtraw), 16)
        p.op("sync", lambda s: s.dma_start(out=d2[:, 0:1], in_=dtb), 16)
        p.op("sync", lambda s: s.dma_start(out=d2[:, 1:2], in_=alog), 16)
        p.op("vector", lambda v: v.tensor_scalar(dr[:], dr[:], d2[:, 0:1], None, ALU.add))
        p.op("scalar", lambda sc_: sc_.activation(d1[:], dr[:], AF.Abs))
        p.op("scalar", lambda sc_: sc_.activation(d1[:], d1[:], AF.Exp, scale=-1.0))
        p.op("scalar", lambda sc_: sc_.activation(d1[:], d1[:], AF.Ln, bias=1.0))
        p.op("vector", lambda v: v.tensor_scalar(dr[:], dr[:], 0.0, None, ALU.max))
        p.op("vector", lambda v: v.tensor_tensor(dr[:], dr[:], d1[:], ALU.add))
        p.op("vector", lambda v: v.tensor_copy(dt[:], dr[:, HALO:T]))
        p.op("scalar", lambda sc_: sc_.activation(d2[:, 2:3], d2[:, 1:2], AF.Exp))
        p.op("vector", lambda v: v.tensor_scalar(d2[:, 3:4], d2[:, 2:3], -1.0, None, ALU.mult))
        p.op("vector", lambda v: v.tensor_scalar(aa[:], dt[:], d2[:, 3:4], None, ALU.mult))
        p.op("gpsimd", lambda g: g.memset(zz[:], 0.0))
        for c in range(NCH):
            p.op("vector", lambda v, c=c: v.tensor_tensor_scan(
                acu[:, c*Q:(c+1)*Q], aa[:, c*Q:(c+1)*Q], zz[:], 0.0, ALU.add, ALU.add))
            p.op("vector", lambda v, c=c: v.tensor_copy(lamc[:, c:c+1], acu[:, (c+1)*Q-1:(c+1)*Q]))
            p.op("vector", lambda v, c=c: v.tensor_scalar(tq[:, c*Q:(c+1)*Q], acu[:, c*Q:(c+1)*Q], lamc[:, c:c+1], None, ALU.subtract))
        p.op("scalar", lambda sc_: sc_.activation(tq[:], tq[:], AF.Exp, scale=-1.0))
        p.op("vector", lambda v: v.tensor_tensor(tq[:], tq[:], dt[:], ALU.mult))
        p.op("sync", lambda s: s.dma_start(out=odtr, in_=tq[:]), 16)
        p.op("sync", lambda s: s.dma_start(out=odt, in_=dt[:]), 16)
        p.op("sync", lambda s: s.dma_start(out=oacum, in_=acu[:]), 16)
        p.op("scalar", lambda sc_: sc_.activation(aa[:], acu[:], AF.Exp))
        p.op("sync", lambda s: s.dma_start(out=oeac, in_=aa[:]), 16)
        p.op("scalar", lambda sc_: sc_.activation(sc[:, 0:NCH], lamc[:], AF.Exp))
        p.op("sync", lambda s: s.dma_start(out=ocd, in_=sc[:, 0:NCH]), 16)
        p.op("vector", lambda v: v.reduce_sum(sc[:, 0:1], lamc[:], axis=mybir.AxisListType.X))
        p.op("sync", lambda s: s.dma_start(out=olam, in_=sc[:, 0:1]), 16)
        p.finalize(nc, block, sem)
    return nc


def build_scan():
    nc = bass.Bass("TRN2", target_bir_lowering=False)
    xsn = nc.dram_tensor("xsn", [NCH, 128, D], BF16, kind="ExternalInput").ap()
    xst = nc.dram_tensor("xst", [D, TK], BF16, kind="ExternalInput").ap()
    bn = nc.dram_tensor("bn", [NCH, 128, S], BF16, kind="ExternalInput").ap()
    bt = nc.dram_tensor("bt", [S, TK], BF16, kind="ExternalInput").ap()
    ct = nc.dram_tensor("ct", [S, TK], BF16, kind="ExternalInput").ap()
    acn = nc.dram_tensor("acn", [NCH, 128, NH], F32, kind="ExternalInput").ap()
    dtn = nc.dram_tensor("dtn", [NCH, 128, NH], F32, kind="ExternalInput").ap()
    drn = nc.dram_tensor("drn", [NCH, 128, NH], F32, kind="ExternalInput").ap()
    acb = nc.dram_tensor("acb", [NCH, 128, NH * Q], F32, kind="ExternalInput").ap()
    cdb = nc.dram_tensor("cdb", [128, NCH * NH], F32, kind="ExternalInput").ap()
    dpr = nc.dram_tensor("dpr", [64, 64], F32, kind="ExternalInput").ap()
    tri = nc.dram_tensor("tri", [128, 128], BF16, kind="ExternalInput").ap()
    oy = nc.dram_tensor("oy", [D, TK], BF16, kind="ExternalOutput").ap()
    osl = nc.dram_tensor("osl", [NCH, 128, D], BF16, kind="ExternalOutput").ap()
    ohe = nc.dram_tensor("ohe", [128, D], F32, kind="ExternalOutput").ap()
    with contextlib.ExitStack() as _es:
        xsc = _es.enter_context(nc.sbuf_tensor("xsc", [128, D], BF16))
        xtc = _es.enter_context(nc.sbuf_tensor("xtc", [64, Q], BF16))
        bnc = _es.enter_context(nc.sbuf_tensor("bnc", [128, S], BF16))
        btc = _es.enter_context(nc.sbuf_tensor("btc", [S, Q], BF16))
        ctc = _es.enter_context(nc.sbuf_tensor("ctc", [S, Q], BF16))
        acnc = _es.enter_context(nc.sbuf_tensor("acnc", [128, NH], F32))
        dtnc = _es.enter_context(nc.sbuf_tensor("dtnc", [128, NH], F32))
        drnc = _es.enter_context(nc.sbuf_tensor("drnc", [128, NH], F32))
        acbc = _es.enter_context(nc.sbuf_tensor("acbc", [128, NH * Q], F32))
        cdbt = _es.enter_context(nc.sbuf_tensor("cdbt", [128, NCH * NH], F32))
        dprt = _es.enter_context(nc.sbuf_tensor("dprt", [64, 64], F32))
        trit = _es.enter_context(nc.sbuf_tensor("trit", [128, 128], BF16))
        g0m = _es.enter_context(nc.sbuf_tensor("g0m", [128, 128], BF16))
        t1 = _es.enter_context(nc.sbuf_tensor("t1", [128, Q], F32))
        eb = _es.enter_context(nc.sbuf_tensor("eb", [128, Q], BF16))
        gam = _es.enter_context(nc.sbuf_tensor("gam", [128, Q], BF16))
        xtl = _es.enter_context(nc.sbuf_tensor("xtl", [128, HD], BF16))
        slc = _es.enter_context(nc.sbuf_tensor("slc", [128, D], BF16))
        hcur = _es.enter_context(nc.sbuf_tensor("hcur", [128, D], F32))
        yt = _es.enter_context(nc.sbuf_tensor("yt", [64, Q], BF16))
        psg = _es.enter_context(nc.psum_tensor("psg", [128, 128], F32))
        psy = _es.enter_context(nc.psum_tensor("psy", [64, 128], F32))
        pss = _es.enter_context(nc.psum_tensor("pss", [128, 512], F32))
        sem = _es.enter_context(nc.semaphore("sem"))
        block = _es.enter_context(nc.Block())
        p = P()
        p.op("sync", lambda s: s.dma_start(out=cdbt[:], in_=cdb), 16)
        p.op("sync", lambda s: s.dma_start(out=dprt[:], in_=dpr), 16)
        p.op("sync", lambda s: s.dma_start(out=trit[:], in_=tri), 16)
        p.op("gpsimd", lambda g: g.memset(hcur[:], 0.0))
        for c in range(NCH):
            p.op("sync", lambda s, c=c: s.dma_start(out=xsc[:], in_=xsn[c, :, :]), 16)
            p.op("sync", lambda s, c=c: s.dma_start(out=bnc[:], in_=bn[c, :, :]), 16)
            p.op("sync", lambda s, c=c: s.dma_start(out=btc[:], in_=bt[:, c*Q:(c+1)*Q]), 16)
            p.op("sync", lambda s, c=c: s.dma_start(out=ctc[:], in_=ct[:, c*Q:(c+1)*Q]), 16)
            p.op("sync", lambda s, c=c: s.dma_start(out=acnc[:], in_=acn[c, :, :]), 16)
            p.op("sync", lambda s, c=c: s.dma_start(out=dtnc[:], in_=dtn[c, :, :]), 16)
            p.op("sync", lambda s, c=c: s.dma_start(out=drnc[:], in_=drn[c, :, :]), 16)
            p.op("sync", lambda s, c=c: s.dma_start(out=acbc[:], in_=acb[c, :, :]), 16)
            p.op("tensor", lambda t: t.matmul(psg[:], btc[:], ctc[:], start=True, stop=True))
            p.op("vector", lambda v: v.tensor_tensor(g0m[:], psg[:], trit[:], ALU.mult))
            for h in range(NH):
                p.op("vector", lambda v, h=h: v.tensor_scalar(
                    t1[:], acbc[:, h*Q:(h+1)*Q], acnc[:, h:h+1], 0.0, ALU.subtract, ALU.min))
                p.op("scalar", lambda sc_: sc_.activation(eb[:], t1[:], AF.Exp))
                p.op("vector", lambda v, h=h: v.scalar_tensor_tensor(
                    gam[:], eb[:], dtnc[:, h:h+1], g0m[:], ALU.mult, ALU.mult))
                p.op("tensor", lambda t, h=h: t.matmul(
                    psy[:], xsc[:, h*HD:(h+1)*HD], gam[:], start=True, stop=True))
                p.op("vector", lambda v, h=h: v.tensor_scalar(
                    xtl[:], xsc[:, h*HD:(h+1)*HD], drnc[:, h:h+1], None, ALU.mult))
                p.op("tensor", lambda t, h=h: t.matmul(
                    pss[:, (h % 8)*HD:(h % 8 + 1)*HD], bnc[:], xtl[:], start=True, stop=True))
                if h % 8 == 7:
                    bk = h // 8
                    p.op("vector", lambda v, bk=bk: v.tensor_copy(slc[:, bk*512:(bk+1)*512], pss[:]))
                p.op("sync", lambda s, c=c, h=h: s.dma_start(out=xtc[:], in_=xst[h*64:(h+1)*64, c*Q:(c+1)*Q]), 16)
                p.op("vector", lambda v, h=h: v.scalar_tensor_tensor(
                    yt[:], xtc[:], dprt[:, h:h+1], psy[:], ALU.mult, ALU.add))
                p.op("sync", lambda s, c=c, h=h: s.dma_start(out=oy[h*64:(h+1)*64, c*Q:(c+1)*Q], in_=yt[:]), 16)
            p.op("sync", lambda s, c=c: s.dma_start(out=osl[c, :, :], in_=slc[:]), 16)
            for h in range(NH):
                p.op("vector", lambda v, c=c, h=h: v.scalar_tensor_tensor(
                    hcur[:, h*HD:(h+1)*HD], hcur[:, h*HD:(h+1)*HD], cdbt[:, c*NH+h:c*NH+h+1],
                    slc[:, h*HD:(h+1)*HD], ALU.mult, ALU.add))
        p.op("sync", lambda s: s.dma_start(out=ohe, in_=hcur[:]), 16)
        p.finalize(nc, block, sem)
    return nc


def build_inter():
    nc = bass.Bass("TRN2", target_bir_lowering=False)
    yin = nc.dram_tensor("yin", [D, TK], BF16, kind="ExternalInput").ap()
    sl = nc.dram_tensor("sl", [NCH, 128, D], BF16, kind="ExternalInput").ap()
    ct = nc.dram_tensor("ct", [S, TK], BF16, kind="ExternalInput").ap()
    eab = nc.dram_tensor("eab", [NCH, 128, NH * Q], F32, kind="ExternalInput").ap()
    cdb = nc.dram_tensor("cdb", [128, NCH * NH], F32, kind="ExternalInput").ap()
    hp = nc.dram_tensor("hp", [3, 128, D], F32, kind="ExternalInput").ap()
    lm = nc.dram_tensor("lm", [3, NH, 1], F32, kind="ExternalInput").ap()
    um = nc.dram_tensor("um", [3, 4, NH, 1], F32, kind="ExternalInput").ap()
    oy = nc.dram_tensor("oy", [D, TK], BF16, kind="ExternalOutput").ap()
    with contextlib.ExitStack() as _es:
        yc = _es.enter_context(nc.sbuf_tensor("yc", [64, Q], BF16))
        slc = _es.enter_context(nc.sbuf_tensor("slc", [128, D], BF16))
        ctc = _es.enter_context(nc.sbuf_tensor("ctc", [S, Q], BF16))
        eabc = _es.enter_context(nc.sbuf_tensor("eabc", [128, NH * Q], F32))
        cdbt = _es.enter_context(nc.sbuf_tensor("cdbt", [128, NCH * NH], F32))
        hk = _es.enter_context(nc.sbuf_tensor("hk", [128, D], F32))
        hcur = _es.enter_context(nc.sbuf_tensor("hcur", [128, D], F32))
        hbf = _es.enter_context(nc.sbuf_tensor("hbf", [128, D], BF16))
        lmk = _es.enter_context(nc.sbuf_tensor("lmk", [NH, 8], F32))
        umk = _es.enter_context(nc.sbuf_tensor("umk", [NH, 4], F32))
        wk = _es.enter_context(nc.sbuf_tensor("wk", [NH, 4], F32))
        wrow = _es.enter_context(nc.sbuf_tensor("wrow", [1, NH], F32))
        wb = _es.enter_context(nc.sbuf_tensor("wb", [128, NH], F32))
        cte = _es.enter_context(nc.sbuf_tensor("cte", [S, Q], BF16))
        psy = _es.enter_context(nc.psum_tensor("psy", [64, 128], F32))
        sem = _es.enter_context(nc.semaphore("sem"))
        block = _es.enter_context(nc.Block())
        wrow_dram = nc.dram_tensor("wrow_dram", [NH, 1], F32, kind="Internal").ap()
        p = P()
        p.op("sync", lambda s: s.dma_start(out=cdbt[:], in_=cdb), 16)
        p.op("gpsimd", lambda g: g.memset(hcur[:], 0.0))
        for k in range(3):
            p.op("sync", lambda s, k=k: s.dma_start(out=lmk[:, k:k+1], in_=lm[k, :, :]), 16)
        for k in range(3):
            p.op("gpsimd", lambda g: g.memset(wk[:, 0:1], 0.0))
            for j in range(3):
                p.op("sync", lambda s, k=k, j=j: s.dma_start(out=umk[:, j:j+1], in_=um[k, j, :, :]), 16)
                p.op("vector", lambda v, j=j: v.scalar_tensor_tensor(
                    wk[:, 0:1], lmk[:, j:j+1], umk[:, j:j+1], wk[:, 0:1], ALU.mult, ALU.add))
            p.op("scalar", lambda sc_: sc_.activation(wk[:, 1:2], wk[:, 0:1], AF.Exp))
            p.op("sync", lambda s, k=k: s.dma_start(out=umk[:, 3:4], in_=um[k, 3, :, :]), 16)
            p.op("vector", lambda v: v.tensor_scalar(wk[:, 2:3], wk[:, 1:2], umk[:, 3:4], None, ALU.mult))
            p.op("sync", lambda s: s.dma_start(out=wrow_dram, in_=wk[:, 2:3]), 16)
            p.op("sync", lambda s: s.dma_start(out=wb[:], in_=wrow_dram.rearrange("h one -> one h").partition_broadcast(128)), 16)
            p.op("sync", lambda s, k=k: s.dma_start(out=hk[:], in_=hp[k, :, :]), 16)
            for h in range(NH):
                p.op("vector", lambda v, h=h: v.scalar_tensor_tensor(
                    hcur[:, h*HD:(h+1)*HD], hk[:, h*HD:(h+1)*HD], wb[:, h:h+1],
                    hcur[:, h*HD:(h+1)*HD], ALU.mult, ALU.add))
        for c in range(NCH):
            p.op("sync", lambda s, c=c: s.dma_start(out=ctc[:], in_=ct[:, c*Q:(c+1)*Q]), 16)
            p.op("sync", lambda s, c=c: s.dma_start(out=eabc[:], in_=eab[c, :, :]), 16)
            p.op("sync", lambda s, c=c: s.dma_start(out=slc[:], in_=sl[c, :, :]), 16)
            p.op("vector", lambda v: v.tensor_copy(hbf[:], hcur[:]))
            for h in range(NH):
                p.op("vector", lambda v, h=h: v.tensor_tensor(
                    cte[:], ctc[:], eabc[0:S, h*Q:(h+1)*Q], ALU.mult))
                p.op("tensor", lambda t, h=h: t.matmul(
                    psy[:], hbf[:, h*HD:(h+1)*HD], cte[:], start=True, stop=True))
                p.op("sync", lambda s, c=c, h=h: s.dma_start(out=yc[:], in_=yin[h*64:(h+1)*64, c*Q:(c+1)*Q]), 16)
                p.op("vector", lambda v: v.tensor_tensor(yc[:], yc[:], psy[:], ALU.add))
                p.op("sync", lambda s, c=c, h=h: s.dma_start(out=oy[h*64:(h+1)*64, c*Q:(c+1)*Q], in_=yc[:]), 16)
            for h in range(NH):
                p.op("vector", lambda v, c=c, h=h: v.scalar_tensor_tensor(
                    hcur[:, h*HD:(h+1)*HD], hcur[:, h*HD:(h+1)*HD], cdbt[:, c*NH+h:c*NH+h+1],
                    slc[:, h*HD:(h+1)*HD], ALU.mult, ALU.add))
        p.finalize(nc, block, sem)
    return nc


def build_gate():
    nc = bass.Bass("TRN2", target_bir_lowering=False)
    yn = nc.dram_tensor("yn", [TK, D], BF16, kind="ExternalInput").ap()
    zn = nc.dram_tensor("zn", [TK, D], BF16, kind="ExternalInput").ap()
    nwb = nc.dram_tensor("nwb", [128, D], F32, kind="ExternalInput").ap()
    og = nc.dram_tensor("og", [TK, D], BF16, kind="ExternalOutput").ap()
    with contextlib.ExitStack() as _es:
        yt = _es.enter_context(nc.sbuf_tensor("yt", [128, D], BF16))
        zt = _es.enter_context(nc.sbuf_tensor("zt", [128, D], BF16))
        nw = _es.enter_context(nc.sbuf_tensor("nw", [128, D], F32))
        g0 = _es.enter_context(nc.sbuf_tensor("g0", [128, D], F32))
        sq = _es.enter_context(nc.sbuf_tensor("sq", [128, D], F32))
        st = _es.enter_context(nc.sbuf_tensor("st", [128, 8], F32))
        ot = _es.enter_context(nc.sbuf_tensor("ot", [128, D], BF16))
        sem = _es.enter_context(nc.semaphore("sem"))
        block = _es.enter_context(nc.Block())
        p = P()
        p.op("sync", lambda s: s.dma_start(out=nw[:], in_=nwb), 16)
        for i in range(8):
            p.op("sync", lambda s, i=i: s.dma_start(out=yt[:], in_=yn[i*128:(i+1)*128, :]), 16)
            p.op("sync", lambda s, i=i: s.dma_start(out=zt[:], in_=zn[i*128:(i+1)*128, :]), 16)
            p.op("scalar", lambda sc_: sc_.activation(zt[:], zt[:], AF.Silu))
            p.op("vector", lambda v: v.tensor_tensor(g0[:], yt[:], zt[:], ALU.mult))
            p.op("vector", lambda v: v.tensor_tensor(sq[:], g0[:], g0[:], ALU.mult))
            p.op("vector", lambda v: v.reduce_sum(st[:, 0:1], sq[:], axis=mybir.AxisListType.X))
            p.op("vector", lambda v: v.tensor_scalar(st[:, 1:2], st[:, 0:1], 1.0 / D, EPS, ALU.mult, ALU.add))
            p.op("scalar", lambda sc_: sc_.activation(st[:, 2:3], st[:, 1:2], AF.Sqrt))
            p.op("vector", lambda v: v.reciprocal(st[:, 3:4], st[:, 2:3]))
            p.op("vector", lambda v: v.tensor_scalar(g0[:], g0[:], st[:, 3:4], None, ALU.mult))
            p.op("vector", lambda v: v.tensor_tensor(ot[:], g0[:], nw[:], ALU.mult))
            p.op("sync", lambda s, i=i: s.dma_start(out=og[i*128:(i+1)*128, :], in_=ot[:]), 16)
        p.finalize(nc, block, sem)
    return nc


_cache = {}

def _get(name, fn):
    if name not in _cache:
        _cache[name] = fn()
    return _cache[name]


def kernel(L0_attn_hidden_states, L0_attn_residual, L0_attn_ln_w, L0_attn_in_proj_w,
           L0_attn_conv_w, L0_attn_A_log, L0_attn_dt_bias, L0_attn_D, L0_attn_norm_w,
           L0_attn_out_proj_w, L0_attn_mup_vector, L0_ffn_w_idx, L0_ffn_absmax, L0_ffn_bias):
    f32 = np.float32
    x0 = np.asarray(L0_attn_hidden_states, f32)
    widx = np.asarray(L0_ffn_w_idx, np.int32)
    absmax = np.asarray(L0_ffn_absmax, f32)
    fbias = np.asarray(L0_ffn_bias, f32)
    wp = np.asarray(L0_attn_in_proj_w, f32)
    wo = np.asarray(L0_attn_out_proj_w, f32)
    convw = np.asarray(L0_attn_conv_w, f32)
    alog = np.asarray(L0_attn_A_log, f32)
    dtb = np.asarray(L0_attn_dt_bias, f32)
    Dv = np.asarray(L0_attn_D, f32)
    lnw = np.asarray(L0_attn_ln_w, f32)
    nrmw = np.asarray(L0_attn_norm_w, f32)
    mup = np.asarray(L0_attn_mup_vector, f32)

    cores = []
    for c in range(NC):
        b, s4 = c // 4, c % 4
        t0 = s4 * TK
        idxs = (np.arange(t0 - HALO, t0 + TK)) % 4096
        cores.append((b, t0, idxs))

    nc1 = _get("deq", build_deq)
    amx_full = np.repeat(absmax.reshape(D, 64), 64, axis=1).astype(f32)
    maps = []
    for c in range(NC):
        sl = slice(c * 512, (c + 1) * 512)
        maps.append({"widx": np.ascontiguousarray(widx[sl]), "amx": np.ascontiguousarray(amx_full[sl])})
    r = _run(nc1, maps)
    Wf = np.concatenate([np.asarray(r[c]["wout"]) for c in range(NC)], axis=0)
    WT = np.ascontiguousarray(Wf.T)
    wt_blk = np.stack([to_blocked(WT[:, m*128:(m+1)*128]) for m in range(32)])

    ncln = _get("ln", lambda: build_ln("ln"))
    maps = [{"xin": np.ascontiguousarray(x0[b][ix])} for (b, t0, ix) in cores]
    r = _run(ncln, maps)
    xln1 = [np.asarray(r[c]["xout"]) for c in range(NC)]

    def pack_act(xn):
        return to_blocked(np.ascontiguousarray(xn.T))

    bcol = np.zeros((128, 32), f32)
    bcol[:, :] = fbias.reshape(32, 128).T

    ncm1 = _get("mm_ffn1", lambda: build_mm("ffn1"))
    maps = []
    for c, (b, t0, ix) in enumerate(cores):
        maps.append({"wblk": wt_blk, "xblk": pack_act(xln1[c]),
                     "res": np.ascontiguousarray(x0[b][ix].T), "bcol": bcol})
    r = _run(ncm1, maps)
    x1T = [np.asarray(r[c]["x1"]) for c in range(NC)]

    ncln2 = _get("lnrms", lambda: build_ln("lnrms"))
    lnwb = np.broadcast_to(lnw, (128, D)).astype(f32)
    maps = [{"xin": np.ascontiguousarray(x1T[c].T), "lnwb": lnwb} for c in range(NC)]
    r = _run(ncln2, maps)
    xm = [np.asarray(r[c]["xout"]) for c in range(NC)]

    ncm2 = _get("mm_inproj", lambda: build_mm("inproj"))
    wp_blk_list = []
    for m in range(66):
        wp_blk_list.append(to_blocked(wp[:, m*128:(m+1)*128]))
    wp_blk_list.append(np.pad(to_blocked(wp[:, 8448:8512]), ((0, 0), (0, 32 * 64))))
    wp_blk = np.stack([np.pad(bb, ((0, 0), (0, 4096 - bb.shape[1]))) for bb in wp_blk_list])
    mcol = np.zeros((128, 67), f32)
    mupp = np.pad(mup, (0, 67 * 128 - PROJ))
    mcol[:, :] = mupp.reshape(67, 128).T
    maps = [{"wblk": wp_blk, "xblk": pack_act(xm[c]), "mcol": mcol} for c in range(NC)]
    r = _run(ncm2, maps)
    zT = [np.asarray(r[c]["zT"]) for c in range(NC)]
    xbcT = [np.asarray(r[c]["xbcT"]) for c in range(NC)]
    dtraw = [np.asarray(r[c]["dtraw"]) for c in range(NC)]

    nc5 = _get("convdt", build_convdt)
    cwt = np.zeros((128, 34 * 4), f32)
    cw_r = convw.reshape(34, 128, 4)
    for i in range(34):
        cwt[:, i*4:(i+1)*4] = cw_r[i]
    maps = []
    for c in range(NC):
        hm = np.full((128, 1), 0.0 if c % 4 == 0 else 1.0, f32)
        maps.append({"xbcT": xbcT[c], "cw": cwt, "halo": hm, "dtraw": dtraw[c],
                     "dtb": dtb.reshape(NH, 1), "alog": alog.reshape(NH, 1)})
    r = _run(nc5, maps)
    xbc = [np.asarray(r[c]["oxbc"]) for c in range(NC)]
    acum = [np.asarray(r[c]["oacum"]) for c in range(NC)]
    eac = [np.asarray(r[c]["oeac"]) for c in range(NC)]
    dtr = [np.asarray(r[c]["odtr"]) for c in range(NC)]
    dtv = [np.asarray(r[c]["odt"]) for c in range(NC)]
    cd = [np.asarray(r[c]["ocd"]) for c in range(NC)]
    lam = [np.asarray(r[c]["olam"]) for c in range(NC)]

    nc6 = _get("scan", build_scan)
    tri = (np.arange(128)[None, :] >= np.arange(128)[:, None]).astype(BF)
    dpr = np.ascontiguousarray(np.broadcast_to(Dv, (64, 64))).astype(f32)
    maps = []
    for c in range(NC):
        xsT = xbc[c][:D]
        bT = xbc[c][D:D+S]
        cT = xbc[c][D+S:D+2*S]
        xs_nat = np.ascontiguousarray(xsT.T).reshape(NCH, 128, D)
        b_nat = np.ascontiguousarray(bT.T).reshape(NCH, 128, S)
        ac_nat = np.ascontiguousarray(acum[c].T).astype(f32).reshape(NCH, 128, NH)
        dt_nat = np.ascontiguousarray(dtv[c].T).astype(f32).reshape(NCH, 128, NH)
        dr_nat = np.ascontiguousarray(dtr[c].T).astype(f32).reshape(NCH, 128, NH)
        acb = np.broadcast_to(acum[c].reshape(NH, NCH, Q).transpose(1, 0, 2).reshape(NCH, 1, NH * Q),
                              (NCH, 128, NH * Q)).astype(f32)
        cdb = np.broadcast_to(cd[c].T.reshape(1, NCH * NH), (128, NCH * NH)).astype(f32)
        maps.append({"xsn": np.ascontiguousarray(xs_nat), "xst": np.ascontiguousarray(xsT),
                     "bn": np.ascontiguousarray(b_nat), "bt": np.ascontiguousarray(bT),
                     "ct": np.ascontiguousarray(cT),
                     "acn": ac_nat, "dtn": dt_nat, "drn": dr_nat,
                     "acb": np.ascontiguousarray(acb), "cdb": np.ascontiguousarray(cdb),
                     "dpr": dpr, "tri": tri})
    r = _run(nc6, maps)
    yintra = [np.asarray(r[c]["oy"]) for c in range(NC)]
    sloc = [np.asarray(r[c]["osl"]) for c in range(NC)]
    hend = [np.asarray(r[c]["ohe"]) for c in range(NC)]

    nc7 = _get("inter", build_inter)
    maps = []
    for c in range(NC):
        g = c % 4
        base = (c // 4) * 4
        hp = np.zeros((3, 128, D), f32)
        lmv = np.zeros((3, NH, 1), f32)
        um = np.zeros((3, 4, NH, 1), f32)
        for k in range(3):
            if k < g:
                hp[k] = hend[base + k]
                lmv[k] = lam[base + k].reshape(NH, 1)
                um[k, 3] = 1.0
                for j in range(3):
                    if k < j < g:
                        um[k, j] = 1.0
        cT = xbc[c][D+S:D+2*S]
        eab = np.broadcast_to(eac[c].reshape(NH, NCH, Q).transpose(1, 0, 2).reshape(NCH, 1, NH * Q),
                              (NCH, 128, NH * Q)).astype(f32)
        cdb = np.broadcast_to(cd[c].T.reshape(1, NCH * NH), (128, NCH * NH)).astype(f32)
        maps.append({"yin": yintra[c], "sl": sloc[c], "ct": np.ascontiguousarray(cT),
                     "eab": np.ascontiguousarray(eab), "cdb": np.ascontiguousarray(cdb),
                     "hp": hp, "lm": lmv, "um": um})
    r = _run(nc7, maps)
    ytot = [np.asarray(r[c]["oy"]) for c in range(NC)]

    nc8 = _get("gate", build_gate)
    nwb = np.broadcast_to(nrmw, (128, D)).astype(f32)
    maps = []
    for c in range(NC):
        maps.append({"yn": np.ascontiguousarray(ytot[c].T), "zn": np.ascontiguousarray(zT[c][:, HALO:].T),
                     "nwb": nwb})
    r = _run(nc8, maps)
    gn = [np.asarray(r[c]["og"]) for c in range(NC)]

    ncm3 = _get("mm_outproj", lambda: build_mm("outproj"))
    wo_blk = np.stack([to_blocked(wo[:, m*128:(m+1)*128]) for m in range(32)])
    maps = []
    for c in range(NC):
        gfull = np.zeros((T, D), BF)
        gfull[HALO:] = gn[c]
        maps.append({"wblk": wo_blk, "xblk": pack_act(gfull), "res": x1T[c]})
    r = _run(ncm3, maps)
    x2T = [np.asarray(r[c]["x2"]) for c in range(NC)]

    maps = [{"xin": np.ascontiguousarray(x2T[c].T)} for c in range(NC)]
    r = _run(ncln, maps)
    xln3 = [np.asarray(r[c]["xout"]) for c in range(NC)]

    ncm4 = _get("mm_ffn2", lambda: build_mm("ffn2"))
    maps = []
    for c in range(NC):
        maps.append({"wblk": wt_blk, "xblk": pack_act(xln3[c]), "res": x2T[c], "bcol": bcol})
    r = _run(ncm4, maps)
    x3T = [np.asarray(r[c]["x3"]) for c in range(NC)]

    out = np.empty((2, 4096, D), f32)
    for c, (b, t0, ix) in enumerate(cores):
        out[b, t0:t0+TK] = x3T[c][:, HALO:].T
    return out


# version 07
# speedup vs baseline: 2.8343x; 1.5862x over previous
import contextlib
import numpy as np
import ml_dtypes
import concourse.bass as bass
import concourse.mybir as mybir
from concourse.bass_utils import run_bass_kernel_spmd

F32 = mybir.dt.float32
BF16 = mybir.dt.bfloat16
I32 = mybir.dt.int32
AF = mybir.ActivationFunctionType
ALU = mybir.AluOpType
BF = ml_dtypes.bfloat16

D = 4096
PROJ = 8512
CONV = 4352
NH = 64
HD = 64
S = 128
Q = 128
NC = 8
TK = 1024
HALO = 8
T = TK + HALO
NCH = 8
EPS = 1e-5

NF4 = np.array([-1.0, -0.6961928009986877, -0.5250730514526367, -0.39491748809814453,
                -0.28444138169288635, -0.18477343022823334, -0.09105003625154495, 0.0,
                0.07958029955625534, 0.16093020141124725, 0.24611230194568634,
                0.33791524171829224, 0.44070982933044434, 0.5626170039176941,
                0.7229568362236023, 1.0], dtype=np.float64)
DELTA = np.diff(NF4)

TSPLITS = [(0, 344), (344, 344), (688, 344)]


class P:
    def __init__(self):
        self.ops = []
        self.cur = 0

    def op(self, eng, fn, inc=1):
        self.ops.append((eng, fn, self.cur, inc))
        self.cur += inc

    def finalize(self, nc, block, sem):
        for name in ["sync", "gpsimd", "vector", "scalar", "tensor"]:
            mine = [(f, at, inc) for (e, f, at, inc) in self.ops if e == name]
            if not mine and name != "gpsimd":
                continue
            deco = getattr(block, name)
            def body(eng, mine=mine, name=name):
                for f, at, inc in mine:
                    eng.wait_ge(sem, at)
                    f(eng).then_inc(sem, inc)
                if name == "gpsimd":
                    eng.wait_ge(sem, self.cur)
            deco(body)


def _run(nc, in_maps):
    res = run_bass_kernel_spmd(nc, in_maps, core_ids=list(range(NC)))
    return res.results


def to_blocked(x2d, tile=128):
    K, N = x2d.shape
    kt = K // tile
    return np.ascontiguousarray(x2d.reshape(kt, tile, N).transpose(1, 0, 2).reshape(tile, kt * N))


def from_blocked(b, K, N, tile=128):
    kt = K // tile
    return np.ascontiguousarray(b.reshape(tile, kt, N).transpose(1, 0, 2).reshape(K, N))


def build_deq():
    nc = bass.Bass("TRN2", target_bir_lowering=False)
    widx = nc.dram_tensor("widx", [512, D], I32, kind="ExternalInput").ap()
    amx = nc.dram_tensor("amx", [512, D], F32, kind="ExternalInput").ap()
    wout = nc.dram_tensor("wout", [512, D], BF16, kind="ExternalOutput").ap()
    with contextlib.ExitStack() as _es:
        idx = _es.enter_context(nc.sbuf_tensor("idx", [128, D], BF16))
        am = _es.enter_context(nc.sbuf_tensor("am", [128, D], F32))
        acc = _es.enter_context(nc.sbuf_tensor("acc", [128, D], F32))
        tmp = _es.enter_context(nc.sbuf_tensor("tmp", [128, D], F32))
        w = _es.enter_context(nc.sbuf_tensor("w", [128, D], BF16))
        sem = _es.enter_context(nc.semaphore("sem"))
        block = _es.enter_context(nc.Block())
        p = P()
        for i in range(4):
            p.op("gpsimd", lambda g, i=i: g.dma_start(out=idx[:], in_=widx[i*128:(i+1)*128, :]), 16)
            p.op("sync", lambda s, i=i: s.dma_start(out=am[:], in_=amx[i*128:(i+1)*128, :]), 16)
            p.op("vector", lambda v: v.tensor_scalar(acc[:], idx[:], 0.5, float(DELTA[0]), ALU.is_ge, ALU.mult))
            for k in range(2, 16):
                p.op("vector", lambda v, k=k: v.tensor_scalar(tmp[:], idx[:], k - 0.5, float(DELTA[k-1]), ALU.is_ge, ALU.mult))
                p.op("vector", lambda v: v.tensor_tensor(acc[:], acc[:], tmp[:], ALU.add))
            p.op("vector", lambda v: v.scalar_tensor_tensor(w[:], acc[:], -1.0, am[:], ALU.add, ALU.mult))
            p.op("sync", lambda s, i=i: s.dma_start(out=wout[i*128:(i+1)*128, :], in_=w[:]), 16)
        p.finalize(nc, block, sem)
    return nc


def build_ln(mode):
    nc = bass.Bass("TRN2", target_bir_lowering=False)
    xin = nc.dram_tensor("xin", [T, D], F32, kind="ExternalInput").ap()
    if mode == "lnrms":
        lnwb = nc.dram_tensor("lnwb", [128, D], F32, kind="ExternalInput").ap()
    xout = nc.dram_tensor("xout", [T, D], BF16, kind="ExternalOutput").ap()
    ntile = [(i * 128, 128) for i in range(8)] + [(1024, 8)]
    with contextlib.ExitStack() as _es:
        xt = _es.enter_context(nc.sbuf_tensor("xt", [128, D], F32))
        sq = _es.enter_context(nc.sbuf_tensor("sq", [128, D], F32))
        lw = _es.enter_context(nc.sbuf_tensor("lw", [128, D], F32))
        st = _es.enter_context(nc.sbuf_tensor("st", [128, 8], F32))
        ot = _es.enter_context(nc.sbuf_tensor("ot", [128, D], BF16))
        sem = _es.enter_context(nc.semaphore("sem"))
        block = _es.enter_context(nc.Block())
        p = P()
        if mode == "lnrms":
            p.op("sync", lambda s: s.dma_start(out=lw[:], in_=lnwb), 16)
        for (r0, rn) in ntile:
            p.op("sync", lambda s, r0=r0, rn=rn: s.dma_start(out=xt[0:rn, :], in_=xin[r0:r0+rn, :]), 16)
            p.op("vector", lambda v, rn=rn: v.reduce_sum(st[0:rn, 0:1], xt[0:rn, :], axis=mybir.AxisListType.X))
            p.op("vector", lambda v, rn=rn: v.tensor_scalar(st[0:rn, 1:2], st[0:rn, 0:1], 1.0 / D, None, ALU.mult))
            p.op("vector", lambda v, rn=rn: v.tensor_scalar(xt[0:rn, :], xt[0:rn, :], st[0:rn, 1:2], None, ALU.subtract))
            p.op("vector", lambda v, rn=rn: v.tensor_tensor(sq[0:rn, :], xt[0:rn, :], xt[0:rn, :], ALU.mult))
            p.op("vector", lambda v, rn=rn: v.reduce_sum(st[0:rn, 2:3], sq[0:rn, :], axis=mybir.AxisListType.X))
            p.op("vector", lambda v, rn=rn: v.tensor_scalar(st[0:rn, 3:4], st[0:rn, 2:3], 1.0 / D, EPS, ALU.mult, ALU.add))
            p.op("scalar", lambda sc, rn=rn: sc.activation(st[0:rn, 4:5], st[0:rn, 3:4], AF.Sqrt))
            p.op("vector", lambda v, rn=rn: v.reciprocal(st[0:rn, 5:6], st[0:rn, 4:5]))
            if mode == "ln":
                p.op("vector", lambda v, rn=rn: v.tensor_scalar(ot[0:rn, :], xt[0:rn, :], st[0:rn, 5:6], None, ALU.mult))
            else:
                p.op("vector", lambda v, rn=rn: v.tensor_scalar(xt[0:rn, :], xt[0:rn, :], st[0:rn, 5:6], None, ALU.mult))
                p.op("vector", lambda v, rn=rn: v.tensor_tensor(sq[0:rn, :], xt[0:rn, :], xt[0:rn, :], ALU.mult))
                p.op("vector", lambda v, rn=rn: v.reduce_sum(st[0:rn, 2:3], sq[0:rn, :], axis=mybir.AxisListType.X))
                p.op("vector", lambda v, rn=rn: v.tensor_scalar(st[0:rn, 3:4], st[0:rn, 2:3], 1.0 / D, EPS, ALU.mult, ALU.add))
                p.op("scalar", lambda sc, rn=rn: sc.activation(st[0:rn, 4:5], st[0:rn, 3:4], AF.Sqrt))
                p.op("vector", lambda v, rn=rn: v.reciprocal(st[0:rn, 5:6], st[0:rn, 4:5]))
                p.op("vector", lambda v, rn=rn: v.tensor_scalar(xt[0:rn, :], xt[0:rn, :], st[0:rn, 5:6], None, ALU.mult))
                p.op("vector", lambda v, rn=rn: v.tensor_tensor(ot[0:rn, :], xt[0:rn, :], lw[0:rn, :], ALU.mult))
            p.op("sync", lambda s, r0=r0, rn=rn: s.dma_start(out=xout[r0:r0+rn, :], in_=ot[0:rn, :]), 16)
        p.finalize(nc, block, sem)
    return nc


def build_mm(mode):
    nc = bass.Bass("TRN2", target_bir_lowering=False)
    KT = 32
    if mode == "inproj":
        mtiles = [128] * 66 + [64]
    else:
        mtiles = [128] * 32
    NM = len(mtiles)
    has_res = mode in ("ffn1", "ffn2", "outproj")
    wdt = BF16 if mode in ("ffn1", "ffn2") else F32
    wblk = nc.dram_tensor("wblk", [NM, 128, D], wdt, kind="ExternalInput").ap()
    xblk = nc.dram_tensor("xblk", [128, KT * T], BF16, kind="ExternalInput").ap()
    outs = {}
    if mode in ("ffn1", "ffn2"):
        res = nc.dram_tensor("res", [D, T], F32, kind="ExternalInput").ap()
        bcol = nc.dram_tensor("bcol", [128, 32], F32, kind="ExternalInput").ap()
        oname = "x1" if mode == "ffn1" else "x3"
        outs[oname] = nc.dram_tensor(oname, [D, T], F32, kind="ExternalOutput").ap()
    elif mode == "outproj":
        res = nc.dram_tensor("res", [D, T], F32, kind="ExternalInput").ap()
        outs["x2"] = nc.dram_tensor("x2", [D, T], F32, kind="ExternalOutput").ap()
    else:
        mcol = nc.dram_tensor("mcol", [128, 67], F32, kind="ExternalInput").ap()
        outs["zT"] = nc.dram_tensor("zT", [D, T], BF16, kind="ExternalOutput").ap()
        outs["xbcT"] = nc.dram_tensor("xbcT", [CONV, T], BF16, kind="ExternalOutput").ap()
        outs["dtraw"] = nc.dram_tensor("dtraw", [NH, T], F32, kind="ExternalOutput").ap()
    with contextlib.ExitStack() as _es:
        xs = _es.enter_context(nc.sbuf_tensor("xs", [128, KT * T], BF16))
        w0 = _es.enter_context(nc.sbuf_tensor("w0", [128, D], BF16))
        w1 = _es.enter_context(nc.sbuf_tensor("w1", [128, D], BF16))
        rt0 = _es.enter_context(nc.sbuf_tensor("rt0", [128, T], F32))
        rt1 = _es.enter_context(nc.sbuf_tensor("rt1", [128, T], F32))
        tm0 = _es.enter_context(nc.sbuf_tensor("tm0", [128, T], F32))
        tm1 = _es.enter_context(nc.sbuf_tensor("tm1", [128, T], F32))
        ob0 = _es.enter_context(nc.sbuf_tensor("ob0", [128, T], BF16))
        ob1 = _es.enter_context(nc.sbuf_tensor("ob1", [128, T], BF16))
        of0 = _es.enter_context(nc.sbuf_tensor("of0", [128, T], F32))
        of1 = _es.enter_context(nc.sbuf_tensor("of1", [128, T], F32))
        cc = _es.enter_context(nc.sbuf_tensor("cc", [128, 67], F32))
        pss = [[_es.enter_context(nc.psum_tensor(f"ps{sdx}{i}", [128, 344], F32))
                for i in range(3)] for sdx in range(2)]
        sx = _es.enter_context(nc.semaphore("sx"))
        sw = _es.enter_context(nc.semaphore("sw"))
        smm = _es.enter_context(nc.semaphore("smm"))
        sev = _es.enter_context(nc.semaphore("sev"))
        srt = _es.enter_context(nc.semaphore("srt"))
        sout = _es.enter_context(nc.semaphore("sout"))
        block = _es.enter_context(nc.Block())
        wb = [w0, w1]
        rts = [rt0, rt1]
        tms = [tm0, tm1]
        obs = [ob0, ob1]
        ofs = [of0, of1]
        n_static = 1 if mode == "outproj" else 2

        @block.sync
        def _(s):
            s.dma_start(out=xs[:], in_=xblk).then_inc(sx, 16)
            if mode == "inproj":
                s.dma_start(out=cc[:], in_=mcol).then_inc(sx, 16)
            elif mode != "outproj":
                s.dma_start(out=cc[:, 0:32], in_=bcol).then_inc(sx, 16)
            if has_res:
                for m in range(NM):
                    if m >= 2:
                        s.wait_ge(sev, m - 1)
                    s.dma_start(out=rts[m % 2][:], in_=res[m*128:(m+1)*128, :]).then_inc(srt, 16)
                    if m >= 1:
                        s.wait_ge(sev, m)
                        mp = m - 1
                        s.dma_start(out=outs[list(outs)[0]][mp*128:(mp+1)*128, :], in_=ofs[mp % 2][:]).then_inc(sout, 16)
                s.wait_ge(sev, NM)
                mp = NM - 1
                s.dma_start(out=outs[list(outs)[0]][mp*128:(mp+1)*128, :], in_=ofs[mp % 2][:]).then_inc(sout, 16)
            else:
                for m, mw in enumerate(mtiles):
                    s.wait_ge(sev, m + 1)
                    if m == 66:
                        s.dma_start(out=outs["dtraw"], in_=ofs[m % 2][0:64, :]).then_inc(sout, 16)
                    elif m < 32:
                        s.dma_start(out=outs["zT"][m*128:(m+1)*128, :], in_=obs[m % 2][:]).then_inc(sout, 16)
                    else:
                        s.dma_start(out=outs["xbcT"][(m-32)*128:(m-31)*128, :], in_=obs[m % 2][:]).then_inc(sout, 16)

        @block.gpsimd
        def _(g):
            for m, mw in enumerate(mtiles):
                if m >= 2:
                    g.wait_ge(smm, m - 1)
                g.dma_start(out=wb[m % 2][0:128, 0:mw*32], in_=wblk[m, :, 0:mw*32]).then_inc(sw, 16)
            g.wait_ge(sout, 16 * NM)

        @block.tensor
        def _(t):
            t.wait_ge(sx, 16 * n_static)
            for m, mw in enumerate(mtiles):
                t.wait_ge(sw, 16 * (m + 1))
                if m >= 2:
                    t.wait_ge(sev, m - 1)
                ps = pss[m % 2]
                last = None
                for kt in range(KT):
                    for i, (t0, tn) in enumerate(TSPLITS):
                        last = t.matmul(ps[i][0:mw, 0:tn], wb[m % 2][:, kt*mw:(kt+1)*mw],
                                        xs[:, kt*T+t0:kt*T+t0+tn],
                                        start=(kt == 0), stop=(kt == KT - 1))
                last.then_inc(smm, 1)

        @block.vector
        def _(v):
            for m, mw in enumerate(mtiles):
                v.wait_ge(smm, m + 1)
                if m >= 2:
                    v.wait_ge(sout, 16 * (m - 1))
                ps = pss[m % 2]
                if mode in ("ffn1", "ffn2"):
                    v.wait_ge(srt, 16 * (m + 1))
                    for i, (t0, tn) in enumerate(TSPLITS):
                        v.tensor_scalar(tms[m % 2][:, t0:t0+tn], ps[i][:, 0:tn], cc[:, m:m+1], 0.5, ALU.add, ALU.mult)
                    v.tensor_tensor(ofs[m % 2][:], tms[m % 2][:], rts[m % 2][:], ALU.add).then_inc(sev, 1)
                elif mode == "outproj":
                    v.wait_ge(srt, 16 * (m + 1))
                    for i, (t0, tn) in enumerate(TSPLITS):
                        last = v.tensor_tensor(ofs[m % 2][:, t0:t0+tn], ps[i][:, 0:tn], rts[m % 2][:, t0:t0+tn], ALU.add)
                    last.then_inc(sev, 1)
                else:
                    tgt = ofs[m % 2] if m == 66 else obs[m % 2]
                    for i, (t0, tn) in enumerate(TSPLITS):
                        last = v.tensor_scalar(tgt[0:mw, t0:t0+tn], ps[i][0:mw, 0:tn], cc[0:mw, m:m+1], None, ALU.mult)
                    last.then_inc(sev, 1)
    return nc


def build_convdt():
    nc = bass.Bass("TRN2", target_bir_lowering=False)
    xbcT = nc.dram_tensor("xbcT", [CONV, T], BF16, kind="ExternalInput").ap()
    cw = nc.dram_tensor("cw", [128, 34 * 4], F32, kind="ExternalInput").ap()
    halo = nc.dram_tensor("halo", [128, 1], F32, kind="ExternalInput").ap()
    dtraw = nc.dram_tensor("dtraw", [NH, T], F32, kind="ExternalInput").ap()
    dtb = nc.dram_tensor("dtb", [NH, 1], F32, kind="ExternalInput").ap()
    alog = nc.dram_tensor("alog", [NH, 1], F32, kind="ExternalInput").ap()
    oxbc = nc.dram_tensor("oxbc", [CONV, TK], BF16, kind="ExternalOutput").ap()
    oacum = nc.dram_tensor("oacum", [NH, TK], F32, kind="ExternalOutput").ap()
    oeac = nc.dram_tensor("oeac", [NH, TK], F32, kind="ExternalOutput").ap()
    odtr = nc.dram_tensor("odtr", [NH, TK], F32, kind="ExternalOutput").ap()
    odt = nc.dram_tensor("odt", [NH, TK], F32, kind="ExternalOutput").ap()
    ocd = nc.dram_tensor("ocd", [NH, NCH], F32, kind="ExternalOutput").ap()
    olam = nc.dram_tensor("olam", [NH, 1], F32, kind="ExternalOutput").ap()
    with contextlib.ExitStack() as _es:
        xt = _es.enter_context(nc.sbuf_tensor("xt", [128, T], BF16))
        cwt = _es.enter_context(nc.sbuf_tensor("cwt", [128, 34 * 4], F32))
        hm = _es.enter_context(nc.sbuf_tensor("hm", [128, 1], F32))
        ac = _es.enter_context(nc.sbuf_tensor("ac", [128, TK], F32))
        ac2 = _es.enter_context(nc.sbuf_tensor("ac2", [128, TK], BF16))
        dr = _es.enter_context(nc.sbuf_tensor("dr", [NH, T], F32))
        d1 = _es.enter_context(nc.sbuf_tensor("d1", [NH, T], F32))
        d2 = _es.enter_context(nc.sbuf_tensor("d2", [NH, T], F32))
        dt = _es.enter_context(nc.sbuf_tensor("dt", [NH, TK], F32))
        aa = _es.enter_context(nc.sbuf_tensor("aa", [NH, TK], F32))
        acu = _es.enter_context(nc.sbuf_tensor("acu", [NH, TK], F32))
        zz = _es.enter_context(nc.sbuf_tensor("zz", [NH, Q], F32))
        lamc = _es.enter_context(nc.sbuf_tensor("lamc", [NH, NCH], F32))
        sc = _es.enter_context(nc.sbuf_tensor("sc", [NH, 8], F32))
        tq = _es.enter_context(nc.sbuf_tensor("tq", [NH, TK], F32))
        sem = _es.enter_context(nc.semaphore("sem"))
        block = _es.enter_context(nc.Block())
        p = P()
        p.op("sync", lambda s: s.dma_start(out=cwt[:], in_=cw), 16)
        p.op("sync", lambda s: s.dma_start(out=hm[:], in_=halo), 16)
        for i in range(34):
            p.op("sync", lambda s, i=i: s.dma_start(out=xt[:], in_=xbcT[i*128:(i+1)*128, :]), 16)
            p.op("vector", lambda v: v.tensor_scalar(xt[:, 0:HALO], xt[:, 0:HALO], hm[:, 0:1], None, ALU.mult))
            p.op("vector", lambda v, i=i: v.tensor_scalar(ac[:], xt[:, 5:5+TK], cwt[:, i*4:i*4+1], None, ALU.mult))
            for k in range(1, 4):
                p.op("vector", lambda v, i=i, k=k: v.scalar_tensor_tensor(
                    ac[:], xt[:, 5+k:5+k+TK], cwt[:, i*4+k:i*4+k+1], ac[:], ALU.mult, ALU.add))
            p.op("scalar", lambda sc_: sc_.activation(ac2[:], ac[:], AF.Silu))
            p.op("sync", lambda s, i=i: s.dma_start(out=oxbc[i*128:(i+1)*128, :], in_=ac2[:]), 16)
        p.op("sync", lambda s: s.dma_start(out=dr[:], in_=dtraw), 16)
        p.op("sync", lambda s: s.dma_start(out=d2[:, 0:1], in_=dtb), 16)
        p.op("sync", lambda s: s.dma_start(out=d2[:, 1:2], in_=alog), 16)
        p.op("vector", lambda v: v.tensor_scalar(dr[:], dr[:], d2[:, 0:1], None, ALU.add))
        p.op("scalar", lambda sc_: sc_.activation(d1[:], dr[:], AF.Abs))
        p.op("scalar", lambda sc_: sc_.activation(d1[:], d1[:], AF.Exp, scale=-1.0))
        p.op("scalar", lambda sc_: sc_.activation(d1[:], d1[:], AF.Ln, bias=1.0))
        p.op("vector", lambda v: v.tensor_scalar(dr[:], dr[:], 0.0, None, ALU.max))
        p.op("vector", lambda v: v.tensor_tensor(dr[:], dr[:], d1[:], ALU.add))
        p.op("vector", lambda v: v.tensor_copy(dt[:], dr[:, HALO:T]))
        p.op("scalar", lambda sc_: sc_.activation(d2[:, 2:3], d2[:, 1:2], AF.Exp))
        p.op("vector", lambda v: v.tensor_scalar(d2[:, 3:4], d2[:, 2:3], -1.0, None, ALU.mult))
        p.op("vector", lambda v: v.tensor_scalar(aa[:], dt[:], d2[:, 3:4], None, ALU.mult))
        p.op("gpsimd", lambda g: g.memset(zz[:], 0.0))
        for c in range(NCH):
            p.op("vector", lambda v, c=c: v.tensor_tensor_scan(
                acu[:, c*Q:(c+1)*Q], aa[:, c*Q:(c+1)*Q], zz[:], 0.0, ALU.add, ALU.add))
            p.op("vector", lambda v, c=c: v.tensor_copy(lamc[:, c:c+1], acu[:, (c+1)*Q-1:(c+1)*Q]))
            p.op("vector", lambda v, c=c: v.tensor_scalar(tq[:, c*Q:(c+1)*Q], acu[:, c*Q:(c+1)*Q], lamc[:, c:c+1], None, ALU.subtract))
        p.op("scalar", lambda sc_: sc_.activation(tq[:], tq[:], AF.Exp, scale=-1.0))
        p.op("vector", lambda v: v.tensor_tensor(tq[:], tq[:], dt[:], ALU.mult))
        p.op("sync", lambda s: s.dma_start(out=odtr, in_=tq[:]), 16)
        p.op("sync", lambda s: s.dma_start(out=odt, in_=dt[:]), 16)
        p.op("sync", lambda s: s.dma_start(out=oacum, in_=acu[:]), 16)
        p.op("scalar", lambda sc_: sc_.activation(aa[:], acu[:], AF.Exp))
        p.op("sync", lambda s: s.dma_start(out=oeac, in_=aa[:]), 16)
        p.op("scalar", lambda sc_: sc_.activation(sc[:, 0:NCH], lamc[:], AF.Exp))
        p.op("sync", lambda s: s.dma_start(out=ocd, in_=sc[:, 0:NCH]), 16)
        p.op("vector", lambda v: v.reduce_sum(sc[:, 0:1], lamc[:], axis=mybir.AxisListType.X))
        p.op("sync", lambda s: s.dma_start(out=olam, in_=sc[:, 0:1]), 16)
        p.finalize(nc, block, sem)
    return nc


def build_scan():
    nc = bass.Bass("TRN2", target_bir_lowering=False)
    xsn = nc.dram_tensor("xsn", [NCH, 128, D], BF16, kind="ExternalInput").ap()
    xst = nc.dram_tensor("xst", [D, TK], BF16, kind="ExternalInput").ap()
    bn = nc.dram_tensor("bn", [NCH, 128, S], BF16, kind="ExternalInput").ap()
    bt = nc.dram_tensor("bt", [S, TK], BF16, kind="ExternalInput").ap()
    ct = nc.dram_tensor("ct", [S, TK], BF16, kind="ExternalInput").ap()
    acn = nc.dram_tensor("acn", [NCH, 128, NH], F32, kind="ExternalInput").ap()
    dtn = nc.dram_tensor("dtn", [NCH, 128, NH], F32, kind="ExternalInput").ap()
    drn = nc.dram_tensor("drn", [NCH, 128, NH], F32, kind="ExternalInput").ap()
    acb = nc.dram_tensor("acb", [NCH, 128, NH * Q], F32, kind="ExternalInput").ap()
    cdb = nc.dram_tensor("cdb", [128, NCH * NH], F32, kind="ExternalInput").ap()
    dpr = nc.dram_tensor("dpr", [64, 64], F32, kind="ExternalInput").ap()
    tri = nc.dram_tensor("tri", [128, 128], BF16, kind="ExternalInput").ap()
    oy = nc.dram_tensor("oy", [D, TK], BF16, kind="ExternalOutput").ap()
    osl = nc.dram_tensor("osl", [NCH, 128, D], BF16, kind="ExternalOutput").ap()
    ohe = nc.dram_tensor("ohe", [128, D], F32, kind="ExternalOutput").ap()
    with contextlib.ExitStack() as _es:
        xsc = _es.enter_context(nc.sbuf_tensor("xsc", [128, D], BF16))
        bnc = _es.enter_context(nc.sbuf_tensor("bnc", [128, S], BF16))
        btc = _es.enter_context(nc.sbuf_tensor("btc", [S, Q], BF16))
        ctc = _es.enter_context(nc.sbuf_tensor("ctc", [S, Q], BF16))
        acnc = _es.enter_context(nc.sbuf_tensor("acnc", [128, NH], F32))
        dtnc = _es.enter_context(nc.sbuf_tensor("dtnc", [128, NH], F32))
        drnc = _es.enter_context(nc.sbuf_tensor("drnc", [128, NH], F32))
        acbc = _es.enter_context(nc.sbuf_tensor("acbc", [128, NH * Q], F32))
        cdbt = _es.enter_context(nc.sbuf_tensor("cdbt", [128, NCH * NH], F32))
        dprt = _es.enter_context(nc.sbuf_tensor("dprt", [64, 64], F32))
        trit = _es.enter_context(nc.sbuf_tensor("trit", [128, 128], BF16))
        g0m = _es.enter_context(nc.sbuf_tensor("g0m", [128, 128], BF16))
        t1b = [_es.enter_context(nc.sbuf_tensor(f"t1_{i}", [128, Q], F32)) for i in range(2)]
        ebb = [_es.enter_context(nc.sbuf_tensor(f"eb_{i}", [128, Q], BF16)) for i in range(2)]
        gmb = [_es.enter_context(nc.sbuf_tensor(f"gm_{i}", [128, Q], BF16)) for i in range(2)]
        xlb = [_es.enter_context(nc.sbuf_tensor(f"xl_{i}", [128, HD], BF16)) for i in range(2)]
        xtb = [_es.enter_context(nc.sbuf_tensor(f"xt_{i}", [64, Q], BF16)) for i in range(2)]
        ytb = [_es.enter_context(nc.sbuf_tensor(f"yt_{i}", [64, Q], BF16)) for i in range(2)]
        slc = _es.enter_context(nc.sbuf_tensor("slc", [128, D], BF16))
        hcur = _es.enter_context(nc.sbuf_tensor("hcur", [128, D], F32))
        psg = _es.enter_context(nc.psum_tensor("psg", [128, 128], F32))
        psyb = [_es.enter_context(nc.psum_tensor(f"psy{i}", [64, 128], F32)) for i in range(2)]
        pssb = [_es.enter_context(nc.psum_tensor(f"pss{i}", [128, 512], F32)) for i in range(2)]
        sld = _es.enter_context(nc.semaphore("sld"))
        sstat = _es.enter_context(nc.semaphore("sstat"))
        sms = _es.enter_context(nc.semaphore("sms"))
        sv1 = _es.enter_context(nc.semaphore("sv1"))
        sa = _es.enter_context(nc.semaphore("sa"))
        svd = _es.enter_context(nc.semaphore("svd"))
        spe = _es.enter_context(nc.semaphore("spe"))
        sevy = _es.enter_context(nc.semaphore("sevy"))
        sg0 = _es.enter_context(nc.semaphore("sg0"))
        sxt = _es.enter_context(nc.semaphore("sxt"))
        souty = _es.enter_context(nc.semaphore("souty"))
        sosl = _es.enter_context(nc.semaphore("sosl"))
        shr = _es.enter_context(nc.semaphore("shr"))
        sohe = _es.enter_context(nc.semaphore("sohe"))
        block = _es.enter_context(nc.Block())

        @block.sync
        def _(s):
            s.dma_start(out=cdbt[:], in_=cdb).then_inc(sstat, 16)
            s.dma_start(out=dprt[:], in_=dpr).then_inc(sstat, 16)
            s.dma_start(out=trit[:], in_=tri).then_inc(sstat, 16)
            for c in range(NCH):
                if c > 0:
                    s.wait_ge(spe, 129 * c)
                    s.wait_ge(svd, 64 * c)
                s.dma_start(out=xsc[:], in_=xsn[c, :, :]).then_inc(sld, 16)
                s.dma_start(out=bnc[:], in_=bn[c, :, :]).then_inc(sld, 16)
                s.dma_start(out=btc[:], in_=bt[:, c*Q:(c+1)*Q]).then_inc(sld, 16)
                s.dma_start(out=ctc[:], in_=ct[:, c*Q:(c+1)*Q]).then_inc(sld, 16)
                s.dma_start(out=acnc[:], in_=acn[c, :, :]).then_inc(sld, 16)
                s.dma_start(out=dtnc[:], in_=dtn[c, :, :]).then_inc(sld, 16)
                s.dma_start(out=drnc[:], in_=drn[c, :, :]).then_inc(sld, 16)
                s.dma_start(out=acbc[:], in_=acb[c, :, :]).then_inc(sld, 16)
                for h in range(NH):
                    g = c * NH + h
                    if g >= 2:
                        s.wait_ge(sevy, g - 1)
                    s.dma_start(out=xtb[h % 2][:], in_=xst[h*64:(h+1)*64, c*Q:(c+1)*Q]).then_inc(sxt, 16)
                    if g >= 1:
                        s.wait_ge(sevy, g)
                        hp, cp = (g - 1) % NH, (g - 1) // NH
                        s.dma_start(out=oy[hp*64:(hp+1)*64, cp*Q:(cp+1)*Q], in_=ytb[hp % 2][:]).then_inc(souty, 16)
                s.wait_ge(sevy, 64 * (c + 1))
                s.dma_start(out=osl[c, :, :], in_=slc[:]).then_inc(sosl, 16)
            s.wait_ge(sevy, 512)
            s.dma_start(out=oy[63*64:64*64, 7*Q:8*Q], in_=ytb[63 % 2][:]).then_inc(souty, 16)
            s.wait_ge(shr, NCH)
            s.dma_start(out=ohe, in_=hcur[:]).then_inc(sohe, 16)

        @block.gpsimd
        def _(g_):
            g_.memset(hcur[:], 0.0).then_inc(sms, 1)
            g_.wait_ge(souty, 16 * 512)
            g_.wait_ge(sosl, 16 * NCH)
            g_.wait_ge(sohe, 16)

        @block.tensor
        def _(t):
            for c in range(NCH):
                t.wait_ge(sld, 128 * (c + 1))
                if c > 0:
                    t.wait_ge(sg0, c)
                t.matmul(psg[:], btc[:], ctc[:], start=True, stop=True).then_inc(spe, 1)
                for h in range(NH):
                    g = c * NH + h
                    t.wait_ge(svd, g + 1)
                    if g >= 2:
                        t.wait_ge(sevy, g - 1)
                    t.matmul(psyb[h % 2][:], xsc[:, h*HD:(h+1)*HD], gmb[h % 2][:], start=True, stop=True).then_inc(spe, 1)
                    t.matmul(pssb[(h // 8) % 2][:, (h % 8)*HD:(h % 8 + 1)*HD], bnc[:], xlb[h % 2][:], start=True, stop=True).then_inc(spe, 1)

        @block.scalar
        def _(sc_):
            for g in range(NCH * NH):
                sc_.wait_ge(sv1, g + 1)
                if g >= 2:
                    sc_.wait_ge(svd, g - 1)
                sc_.activation(ebb[g % 2][:], t1b[g % 2][:], AF.Exp).then_inc(sa, 1)

        @block.vector
        def _(v):
            v.wait_ge(sstat, 48)
            v.wait_ge(sms, 1)
            for c in range(NCH):
                v.wait_ge(spe, 129 * c + 1)
                v.wait_ge(sld, 128 * (c + 1))
                v.tensor_tensor(g0m[:], psg[:], trit[:], ALU.mult).then_inc(sg0, 1)
                for h in range(NH):
                    g = c * NH + h
                    v.tensor_scalar(t1b[h % 2][:], acbc[:, h*Q:(h+1)*Q], acnc[:, h:h+1], 0.0,
                                    ALU.subtract, ALU.min).then_inc(sv1, 1)
                    v.wait_ge(sa, g + 1)
                    v.scalar_tensor_tensor(gmb[h % 2][:], ebb[g % 2][:], dtnc[:, h:h+1], g0m[:],
                                           ALU.mult, ALU.mult)
                    v.tensor_scalar(xlb[h % 2][:], xsc[:, h*HD:(h+1)*HD], drnc[:, h:h+1],
                                    None, ALU.mult).then_inc(svd, 1)
                    if h >= 1:
                        hp = h - 1
                        slot = c * NH + hp
                        v.wait_ge(spe, 129 * c + 1 + 2 * (hp + 1))
                        v.wait_ge(sxt, 16 * (slot + 1))
                        if slot >= 1:
                            v.wait_ge(souty, 16 * (slot - 1))
                        last = v.scalar_tensor_tensor(ytb[hp % 2][:], xtb[hp % 2][:], dprt[:, hp:hp+1],
                                                      psyb[hp % 2][:], ALU.mult, ALU.add)
                        if hp % 8 == 7:
                            if c > 0:
                                v.wait_ge(sosl, 16 * c)
                            bk = hp // 8
                            last = v.tensor_copy(slc[:, bk*512:(bk+1)*512], pssb[bk % 2][:])
                        last.then_inc(sevy, 1)
                g = c * NH + NH - 1
                v.wait_ge(spe, 129 * c + 1 + 2 * NH)
                v.wait_ge(sxt, 16 * (g + 1))
                v.wait_ge(souty, 16 * (g - 1))
                last = v.scalar_tensor_tensor(ytb[63 % 2][:], xtb[63 % 2][:], dprt[:, 63:64],
                                              psyb[63 % 2][:], ALU.mult, ALU.add)
                if c > 0:
                    v.wait_ge(sosl, 16 * c)
                last = v.tensor_copy(slc[:, 7*512:8*512], pssb[7 % 2][:])
                last.then_inc(sevy, 1)
                for h in range(NH):
                    lastr = v.scalar_tensor_tensor(hcur[:, h*HD:(h+1)*HD], hcur[:, h*HD:(h+1)*HD],
                                                   cdbt[:, c*NH+h:c*NH+h+1], slc[:, h*HD:(h+1)*HD],
                                                   ALU.mult, ALU.add)
                lastr.then_inc(shr, 1)
    return nc


def build_inter():
    nc = bass.Bass("TRN2", target_bir_lowering=False)
    yin = nc.dram_tensor("yin", [D, TK], BF16, kind="ExternalInput").ap()
    sl = nc.dram_tensor("sl", [NCH, 128, D], BF16, kind="ExternalInput").ap()
    ct = nc.dram_tensor("ct", [S, TK], BF16, kind="ExternalInput").ap()
    eab = nc.dram_tensor("eab", [NCH, 128, NH * Q], F32, kind="ExternalInput").ap()
    cdb = nc.dram_tensor("cdb", [128, NCH * NH], F32, kind="ExternalInput").ap()
    hp = nc.dram_tensor("hp", [3, 128, D], F32, kind="ExternalInput").ap()
    lm = nc.dram_tensor("lm", [3, NH, 1], F32, kind="ExternalInput").ap()
    um = nc.dram_tensor("um", [3, 4, NH, 1], F32, kind="ExternalInput").ap()
    oy = nc.dram_tensor("oy", [D, TK], BF16, kind="ExternalOutput").ap()
    with contextlib.ExitStack() as _es:
        slc = _es.enter_context(nc.sbuf_tensor("slc", [128, D], BF16))
        ctc = _es.enter_context(nc.sbuf_tensor("ctc", [S, Q], BF16))
        eabc = _es.enter_context(nc.sbuf_tensor("eabc", [128, NH * Q], F32))
        cdbt = _es.enter_context(nc.sbuf_tensor("cdbt", [128, NCH * NH], F32))
        hk = _es.enter_context(nc.sbuf_tensor("hk", [128, D], F32))
        htm = _es.enter_context(nc.sbuf_tensor("htm", [128, D], F32))
        hcur = _es.enter_context(nc.sbuf_tensor("hcur", [128, D], F32))
        hbf = _es.enter_context(nc.sbuf_tensor("hbf", [128, D], BF16))
        lmk = _es.enter_context(nc.sbuf_tensor("lmk", [NH, 8], F32))
        umk = _es.enter_context(nc.sbuf_tensor("umk", [NH, 4], F32))
        wk = _es.enter_context(nc.sbuf_tensor("wk", [NH, 4], F32))
        wb = _es.enter_context(nc.sbuf_tensor("wb", [128, NH], F32))
        cteb = [_es.enter_context(nc.sbuf_tensor(f"cte{i}", [S, Q], BF16)) for i in range(2)]
        ycb = [_es.enter_context(nc.sbuf_tensor(f"yc{i}", [64, Q], BF16)) for i in range(2)]
        psyb = [_es.enter_context(nc.psum_tensor(f"psy{i}", [64, 128], F32)) for i in range(2)]
        spre = _es.enter_context(nc.semaphore("spre"))
        sld = _es.enter_context(nc.semaphore("sld"))
        shb = _es.enter_context(nc.semaphore("shb"))
        svd = _es.enter_context(nc.semaphore("svd"))
        spe = _es.enter_context(nc.semaphore("spe"))
        sevy = _es.enter_context(nc.semaphore("sevy"))
        syc = _es.enter_context(nc.semaphore("syc"))
        souty = _es.enter_context(nc.semaphore("souty"))
        shr = _es.enter_context(nc.semaphore("shr"))
        block = _es.enter_context(nc.Block())
        wrow_dram = nc.dram_tensor("wrow_dram", [NH, 1], F32, kind="Internal").ap()

        pre = P()
        pre.op("sync", lambda s: s.dma_start(out=cdbt[:], in_=cdb), 16)
        pre.op("gpsimd", lambda g: g.memset(hcur[:], 0.0))
        for k in range(3):
            pre.op("sync", lambda s, k=k: s.dma_start(out=lmk[:, k:k+1], in_=lm[k, :, :]), 16)
        for k in range(3):
            pre.op("gpsimd", lambda g: g.memset(wk[:, 0:1], 0.0))
            for j in range(3):
                pre.op("sync", lambda s, k=k, j=j: s.dma_start(out=umk[:, j:j+1], in_=um[k, j, :, :]), 16)
                pre.op("vector", lambda v, j=j: v.scalar_tensor_tensor(
                    wk[:, 0:1], lmk[:, j:j+1], umk[:, j:j+1], wk[:, 0:1], ALU.mult, ALU.add))
            pre.op("scalar", lambda sc_: sc_.activation(wk[:, 1:2], wk[:, 0:1], AF.Exp))
            pre.op("sync", lambda s, k=k: s.dma_start(out=umk[:, 3:4], in_=um[k, 3, :, :]), 16)
            pre.op("vector", lambda v: v.tensor_scalar(wk[:, 2:3], wk[:, 1:2], umk[:, 3:4], None, ALU.mult))
            pre.op("sync", lambda s: s.dma_start(out=wrow_dram, in_=wk[:, 2:3]), 16)
            pre.op("sync", lambda s: s.dma_start(out=wb[:], in_=wrow_dram.rearrange("h one -> one h").partition_broadcast(128)), 16)
            pre.op("sync", lambda s, k=k: s.dma_start(out=hk[:], in_=hp[k, :, :]), 16)
            wrep = wb[:].unsqueeze(2).broadcast_to([128, NH, HD])
            pre.op("vector", lambda v, wrep=wrep: v.tensor_tensor(
                htm[:].rearrange("p (h d) -> p h d", h=NH), hk[:].rearrange("p (h d) -> p h d", h=NH),
                wrep, ALU.mult))
            pre.op("vector", lambda v: v.tensor_tensor(hcur[:], hcur[:], htm[:], ALU.add))
        pre.finalize(nc, block, spre)

        @block.sync
        def _(s):
            s.wait_ge(spre, pre.cur)
            for c in range(NCH):
                if c > 0:
                    s.wait_ge(svd, 64 * c)
                    s.wait_ge(shr, c)
                s.dma_start(out=ctc[:], in_=ct[:, c*Q:(c+1)*Q]).then_inc(sld, 16)
                s.dma_start(out=eabc[:], in_=eab[c, :, :]).then_inc(sld, 16)
                s.dma_start(out=slc[:], in_=sl[c, :, :]).then_inc(sld, 16)
                for h in range(NH):
                    g = c * NH + h
                    if g >= 2:
                        s.wait_ge(souty, 16 * (g - 1))
                    s.dma_start(out=ycb[h % 2][:], in_=yin[h*64:(h+1)*64, c*Q:(c+1)*Q]).then_inc(syc, 16)
                    if g >= 1:
                        gp = g - 1
                        hpx, cpx = gp % NH, gp // NH
                        s.wait_ge(sevy, gp + 1)
                        s.dma_start(out=oy[hpx*64:(hpx+1)*64, cpx*Q:(cpx+1)*Q], in_=ycb[hpx % 2][:]).then_inc(souty, 16)
            s.wait_ge(sevy, 512)
            s.dma_start(out=oy[63*64:64*64, 7*Q:8*Q], in_=ycb[63 % 2][:]).then_inc(souty, 16)

        @block.gpsimd
        def _(g_):
            g_.wait_ge(souty, 16 * 512)

        @block.tensor
        def _(t):
            for c in range(NCH):
                for h in range(NH):
                    g = c * NH + h
                    t.wait_ge(shb, c + 1)
                    t.wait_ge(svd, g + 1)
                    if g >= 2:
                        t.wait_ge(sevy, g - 1)
                    t.matmul(psyb[h % 2][:], hbf[:, h*HD:(h+1)*HD], cteb[h % 2][:],
                             start=True, stop=True).then_inc(spe, 1)

        @block.vector
        def _(v):
            v.wait_ge(spre, pre.cur)
            for c in range(NCH):
                v.wait_ge(sld, 48 * (c + 1))
                v.tensor_copy(hbf[:], hcur[:]).then_inc(shb, 1)
                for h in range(NH):
                    g = c * NH + h
                    if g >= 2:
                        v.wait_ge(spe, g - 1)
                    v.tensor_tensor(cteb[h % 2][:], ctc[:], eabc[0:S, h*Q:(h+1)*Q], ALU.mult).then_inc(svd, 1)
                    if h >= 1:
                        slot = c * NH + h - 1
                        v.wait_ge(spe, slot + 1)
                        v.wait_ge(syc, 16 * (slot + 1))
                        v.tensor_tensor(ycb[(h-1) % 2][:], ycb[(h-1) % 2][:], psyb[(h-1) % 2][:],
                                        ALU.add).then_inc(sevy, 1)
                v.wait_ge(spe, 64 * (c + 1))
                v.wait_ge(syc, 16 * 64 * (c + 1))
                v.tensor_tensor(ycb[63 % 2][:], ycb[63 % 2][:], psyb[63 % 2][:], ALU.add).then_inc(sevy, 1)
                cdrep = cdbt[:, c*NH:(c+1)*NH].unsqueeze(2).broadcast_to([128, NH, HD])
                v.tensor_tensor(hcur[:].rearrange("p (h d) -> p h d", h=NH),
                                hcur[:].rearrange("p (h d) -> p h d", h=NH), cdrep, ALU.mult)
                v.tensor_tensor(hcur[:], hcur[:], slc[:], ALU.add).then_inc(shr, 1)
    return nc


def build_gate():
    nc = bass.Bass("TRN2", target_bir_lowering=False)
    yn = nc.dram_tensor("yn", [TK, D], BF16, kind="ExternalInput").ap()
    zn = nc.dram_tensor("zn", [TK, D], BF16, kind="ExternalInput").ap()
    nwb = nc.dram_tensor("nwb", [128, D], F32, kind="ExternalInput").ap()
    og = nc.dram_tensor("og", [TK, D], BF16, kind="ExternalOutput").ap()
    with contextlib.ExitStack() as _es:
        yt = _es.enter_context(nc.sbuf_tensor("yt", [128, D], BF16))
        zt = _es.enter_context(nc.sbuf_tensor("zt", [128, D], BF16))
        nw = _es.enter_context(nc.sbuf_tensor("nw", [128, D], F32))
        g0 = _es.enter_context(nc.sbuf_tensor("g0", [128, D], F32))
        sq = _es.enter_context(nc.sbuf_tensor("sq", [128, D], F32))
        st = _es.enter_context(nc.sbuf_tensor("st", [128, 8], F32))
        ot = _es.enter_context(nc.sbuf_tensor("ot", [128, D], BF16))
        sem = _es.enter_context(nc.semaphore("sem"))
        block = _es.enter_context(nc.Block())
        p = P()
        p.op("sync", lambda s: s.dma_start(out=nw[:], in_=nwb), 16)
        for i in range(8):
            p.op("sync", lambda s, i=i: s.dma_start(out=yt[:], in_=yn[i*128:(i+1)*128, :]), 16)
            p.op("sync", lambda s, i=i: s.dma_start(out=zt[:], in_=zn[i*128:(i+1)*128, :]), 16)
            p.op("scalar", lambda sc_: sc_.activation(zt[:], zt[:], AF.Silu))
            p.op("vector", lambda v: v.tensor_tensor(g0[:], yt[:], zt[:], ALU.mult))
            p.op("vector", lambda v: v.tensor_tensor(sq[:], g0[:], g0[:], ALU.mult))
            p.op("vector", lambda v: v.reduce_sum(st[:, 0:1], sq[:], axis=mybir.AxisListType.X))
            p.op("vector", lambda v: v.tensor_scalar(st[:, 1:2], st[:, 0:1], 1.0 / D, EPS, ALU.mult, ALU.add))
            p.op("scalar", lambda sc_: sc_.activation(st[:, 2:3], st[:, 1:2], AF.Sqrt))
            p.op("vector", lambda v: v.reciprocal(st[:, 3:4], st[:, 2:3]))
            p.op("vector", lambda v: v.tensor_scalar(g0[:], g0[:], st[:, 3:4], None, ALU.mult))
            p.op("vector", lambda v: v.tensor_tensor(ot[:], g0[:], nw[:], ALU.mult))
            p.op("sync", lambda s, i=i: s.dma_start(out=og[i*128:(i+1)*128, :], in_=ot[:]), 16)
        p.finalize(nc, block, sem)
    return nc


_cache = {}

def _get(name, fn):
    if name not in _cache:
        _cache[name] = fn()
    return _cache[name]


def kernel(L0_attn_hidden_states, L0_attn_residual, L0_attn_ln_w, L0_attn_in_proj_w,
           L0_attn_conv_w, L0_attn_A_log, L0_attn_dt_bias, L0_attn_D, L0_attn_norm_w,
           L0_attn_out_proj_w, L0_attn_mup_vector, L0_ffn_w_idx, L0_ffn_absmax, L0_ffn_bias):
    f32 = np.float32
    x0 = np.asarray(L0_attn_hidden_states, f32)
    widx = np.asarray(L0_ffn_w_idx, np.int32)
    absmax = np.asarray(L0_ffn_absmax, f32)
    fbias = np.asarray(L0_ffn_bias, f32)
    wp = np.asarray(L0_attn_in_proj_w, f32)
    wo = np.asarray(L0_attn_out_proj_w, f32)
    convw = np.asarray(L0_attn_conv_w, f32)
    alog = np.asarray(L0_attn_A_log, f32)
    dtb = np.asarray(L0_attn_dt_bias, f32)
    Dv = np.asarray(L0_attn_D, f32)
    lnw = np.asarray(L0_attn_ln_w, f32)
    nrmw = np.asarray(L0_attn_norm_w, f32)
    mup = np.asarray(L0_attn_mup_vector, f32)

    cores = []
    for c in range(NC):
        b, s4 = c // 4, c % 4
        t0 = s4 * TK
        idxs = (np.arange(t0 - HALO, t0 + TK)) % 4096
        cores.append((b, t0, idxs))

    nc1 = _get("deq", build_deq)
    amx_full = np.repeat(absmax.reshape(D, 64), 64, axis=1).astype(f32)
    maps = []
    for c in range(NC):
        sl = slice(c * 512, (c + 1) * 512)
        maps.append({"widx": np.ascontiguousarray(widx[sl]), "amx": np.ascontiguousarray(amx_full[sl])})
    r = _run(nc1, maps)
    Wf = np.concatenate([np.asarray(r[c]["wout"]) for c in range(NC)], axis=0)
    WT = np.ascontiguousarray(Wf.T)
    wt_blk = np.stack([to_blocked(WT[:, m*128:(m+1)*128]) for m in range(32)])

    ncln = _get("ln", lambda: build_ln("ln"))
    maps = [{"xin": np.ascontiguousarray(x0[b][ix])} for (b, t0, ix) in cores]
    r = _run(ncln, maps)
    xln1 = [np.asarray(r[c]["xout"]) for c in range(NC)]

    def pack_act(xn):
        return to_blocked(np.ascontiguousarray(xn.T))

    bcol = np.zeros((128, 32), f32)
    bcol[:, :] = fbias.reshape(32, 128).T

    ncm1 = _get("mm_ffn1", lambda: build_mm("ffn1"))
    maps = []
    for c, (b, t0, ix) in enumerate(cores):
        maps.append({"wblk": wt_blk, "xblk": pack_act(xln1[c]),
                     "res": np.ascontiguousarray(x0[b][ix].T), "bcol": bcol})
    r = _run(ncm1, maps)
    x1T = [np.asarray(r[c]["x1"]) for c in range(NC)]

    ncln2 = _get("lnrms", lambda: build_ln("lnrms"))
    lnwb = np.broadcast_to(lnw, (128, D)).astype(f32)
    maps = [{"xin": np.ascontiguousarray(x1T[c].T), "lnwb": lnwb} for c in range(NC)]
    r = _run(ncln2, maps)
    xm = [np.asarray(r[c]["xout"]) for c in range(NC)]

    ncm2 = _get("mm_inproj", lambda: build_mm("inproj"))
    wp_blk_list = []
    for m in range(66):
        wp_blk_list.append(to_blocked(wp[:, m*128:(m+1)*128]))
    wp_blk_list.append(np.pad(to_blocked(wp[:, 8448:8512]), ((0, 0), (0, 32 * 64))))
    wp_blk = np.stack([np.pad(bb, ((0, 0), (0, 4096 - bb.shape[1]))) for bb in wp_blk_list])
    mcol = np.zeros((128, 67), f32)
    mupp = np.pad(mup, (0, 67 * 128 - PROJ))
    mcol[:, :] = mupp.reshape(67, 128).T
    maps = [{"wblk": wp_blk, "xblk": pack_act(xm[c]), "mcol": mcol} for c in range(NC)]
    r = _run(ncm2, maps)
    zT = [np.asarray(r[c]["zT"]) for c in range(NC)]
    xbcT = [np.asarray(r[c]["xbcT"]) for c in range(NC)]
    dtraw = [np.asarray(r[c]["dtraw"]) for c in range(NC)]

    nc5 = _get("convdt", build_convdt)
    cwt = np.zeros((128, 34 * 4), f32)
    cw_r = convw.reshape(34, 128, 4)
    for i in range(34):
        cwt[:, i*4:(i+1)*4] = cw_r[i]
    maps = []
    for c in range(NC):
        hm = np.full((128, 1), 0.0 if c % 4 == 0 else 1.0, f32)
        maps.append({"xbcT": xbcT[c], "cw": cwt, "halo": hm, "dtraw": dtraw[c],
                     "dtb": dtb.reshape(NH, 1), "alog": alog.reshape(NH, 1)})
    r = _run(nc5, maps)
    xbc = [np.asarray(r[c]["oxbc"]) for c in range(NC)]
    acum = [np.asarray(r[c]["oacum"]) for c in range(NC)]
    eac = [np.asarray(r[c]["oeac"]) for c in range(NC)]
    dtr = [np.asarray(r[c]["odtr"]) for c in range(NC)]
    dtv = [np.asarray(r[c]["odt"]) for c in range(NC)]
    cd = [np.asarray(r[c]["ocd"]) for c in range(NC)]
    lam = [np.asarray(r[c]["olam"]) for c in range(NC)]

    nc6 = _get("scan", build_scan)
    tri = (np.arange(128)[None, :] >= np.arange(128)[:, None]).astype(BF)
    dpr = np.ascontiguousarray(np.broadcast_to(Dv, (64, 64))).astype(f32)
    maps = []
    for c in range(NC):
        xsT = xbc[c][:D]
        bT = xbc[c][D:D+S]
        cT = xbc[c][D+S:D+2*S]
        xs_nat = np.ascontiguousarray(xsT.T).reshape(NCH, 128, D)
        b_nat = np.ascontiguousarray(bT.T).reshape(NCH, 128, S)
        ac_nat = np.ascontiguousarray(acum[c].T).astype(f32).reshape(NCH, 128, NH)
        dt_nat = np.ascontiguousarray(dtv[c].T).astype(f32).reshape(NCH, 128, NH)
        dr_nat = np.ascontiguousarray(dtr[c].T).astype(f32).reshape(NCH, 128, NH)
        acb = np.broadcast_to(acum[c].reshape(NH, NCH, Q).transpose(1, 0, 2).reshape(NCH, 1, NH * Q),
                              (NCH, 128, NH * Q)).astype(f32)
        cdb = np.broadcast_to(cd[c].T.reshape(1, NCH * NH), (128, NCH * NH)).astype(f32)
        maps.append({"xsn": np.ascontiguousarray(xs_nat), "xst": np.ascontiguousarray(xsT),
                     "bn": np.ascontiguousarray(b_nat), "bt": np.ascontiguousarray(bT),
                     "ct": np.ascontiguousarray(cT),
                     "acn": ac_nat, "dtn": dt_nat, "drn": dr_nat,
                     "acb": np.ascontiguousarray(acb), "cdb": np.ascontiguousarray(cdb),
                     "dpr": dpr, "tri": tri})
    r = _run(nc6, maps)
    yintra = [np.asarray(r[c]["oy"]) for c in range(NC)]
    sloc = [np.asarray(r[c]["osl"]) for c in range(NC)]
    hend = [np.asarray(r[c]["ohe"]) for c in range(NC)]

    nc7 = _get("inter", build_inter)
    maps = []
    for c in range(NC):
        g = c % 4
        base = (c // 4) * 4
        hp = np.zeros((3, 128, D), f32)
        lmv = np.zeros((3, NH, 1), f32)
        um = np.zeros((3, 4, NH, 1), f32)
        for k in range(3):
            if k < g:
                hp[k] = hend[base + k]
                lmv[k] = lam[base + k].reshape(NH, 1)
                um[k, 3] = 1.0
                for j in range(3):
                    if k < j < g:
                        um[k, j] = 1.0
        cT = xbc[c][D+S:D+2*S]
        eab = np.broadcast_to(eac[c].reshape(NH, NCH, Q).transpose(1, 0, 2).reshape(NCH, 1, NH * Q),
                              (NCH, 128, NH * Q)).astype(f32)
        cdb = np.broadcast_to(cd[c].T.reshape(1, NCH * NH), (128, NCH * NH)).astype(f32)
        maps.append({"yin": yintra[c], "sl": sloc[c], "ct": np.ascontiguousarray(cT),
                     "eab": np.ascontiguousarray(eab), "cdb": np.ascontiguousarray(cdb),
                     "hp": hp, "lm": lmv, "um": um})
    r = _run(nc7, maps)
    ytot = [np.asarray(r[c]["oy"]) for c in range(NC)]

    nc8 = _get("gate", build_gate)
    nwb = np.broadcast_to(nrmw, (128, D)).astype(f32)
    maps = []
    for c in range(NC):
        maps.append({"yn": np.ascontiguousarray(ytot[c].T), "zn": np.ascontiguousarray(zT[c][:, HALO:].T),
                     "nwb": nwb})
    r = _run(nc8, maps)
    gn = [np.asarray(r[c]["og"]) for c in range(NC)]

    ncm3 = _get("mm_outproj", lambda: build_mm("outproj"))
    wo_blk = np.stack([to_blocked(wo[:, m*128:(m+1)*128]) for m in range(32)])
    maps = []
    for c in range(NC):
        gfull = np.zeros((T, D), BF)
        gfull[HALO:] = gn[c]
        maps.append({"wblk": wo_blk, "xblk": pack_act(gfull), "res": x1T[c]})
    r = _run(ncm3, maps)
    x2T = [np.asarray(r[c]["x2"]) for c in range(NC)]

    maps = [{"xin": np.ascontiguousarray(x2T[c].T)} for c in range(NC)]
    r = _run(ncln, maps)
    xln3 = [np.asarray(r[c]["xout"]) for c in range(NC)]

    ncm4 = _get("mm_ffn2", lambda: build_mm("ffn2"))
    maps = []
    for c in range(NC):
        maps.append({"wblk": wt_blk, "xblk": pack_act(xln3[c]), "res": x2T[c], "bcol": bcol})
    r = _run(ncm4, maps)
    x3T = [np.asarray(r[c]["x3"]) for c in range(NC)]

    out = np.empty((2, 4096, D), f32)
    for c, (b, t0, ix) in enumerate(cores):
        out[b, t0:t0+TK] = x3T[c][:, HALO:].T
    return out
